# version 08
# baseline (speedup 1.0000x reference)
import numpy as np
import ml_dtypes
import concourse.bass as bass
import concourse.mybir as mybir
from concourse.bass_utils import run_bass_kernel_spmd

F32 = mybir.dt.float32
BF16 = mybir.dt.bfloat16
AF = mybir.ActivationFunctionType
ALU = mybir.AluOpType
AX = mybir.AxisListType
AP = bass.AP

NCORE = 8
DM = 2048
TOK = 1024
PREV = 512
TT_OWN = 8
TT_ALL = 12
DC = 16
H = 32
HPAIR = 16
WIN = 512
DFF = 8192
MSTAB = 20.0
MAGIC = 1.5 * 2 ** 23
LN_EPS = 1e-5
Q_EPS = 1e-5
QB = 127.0
FW = 1536
UW = 1408
SROW = 128 * FW
HFMW = DC * (TOK + PREV)
OFMW = DC * TOK


class Sch:
    ENGS = ("t", "s", "v", "g", "y")

    def __init__(self):
        self.ops = {e: [] for e in self.ENGS}
        self.cnt = {}
        self.last_wait = {}
        self.handles = {}

    def emit(self, eng, fn, inc=None, inc_n=1, waits=()):
        wl = []
        for w in waits:
            if w is None:
                continue
            sn, val = w
            assert val <= self.cnt[sn], f"future token {sn}={val} > {self.cnt[sn]}"
            key = (eng, sn)
            if self.last_wait.get(key, -1) < val:
                self.last_wait[key] = val
                wl.append((sn, val))
        tok = None
        if inc is not None:
            self.cnt[inc] += inc_n
            tok = (inc, self.cnt[inc])
        hs = self.handles

        def op(e, wl=wl, fn=fn, inc=inc, inc_n=inc_n):
            for sn, val in wl:
                e.wait_ge(hs[sn], val)
            inst = fn(e)
            if inc is not None:
                inst.then_inc(hs[inc], inc_n)
        self.ops[eng].append(op)
        return tok


def build(dbg=False, dbg_sel=None):
    nc = bass.Bass("TRN2", target_bir_lowering=False,
                   dynamic_dma_scratch_size=8192,
                   detect_race_conditions=False)

    dt = nc.dram_tensor
    xo_d = dt("xo", [TOK, DM], F32, kind="ExternalInput")
    xp_d = dt("xp", [PREV, DM], F32, kind="ExternalInput")
    pg_d = dt("pg", [128, 1], F32, kind="ExternalInput")
    wq_d = dt("wq", [DM, DM], F32, kind="ExternalInput")
    wk_d = dt("wk", [DM, DM], F32, kind="ExternalInput")
    wv_d = dt("wv", [DM, DM], F32, kind="ExternalInput")
    wo_d = dt("wo", [DM, DM], F32, kind="ExternalInput")
    dog_d = dt("dog", [H, 7], F32, kind="ExternalInput")
    sink_d = dt("sink", [H, 1], F32, kind="ExternalInput")
    w1_d = dt("w1", [DFF, DM], F32, kind="ExternalInput")
    w2_d = dt("w2", [DFF, DM], F32, kind="ExternalInput")
    w3_d = dt("w3", [DM, DFF], F32, kind="ExternalInput")
    c_d = dt("cvec", [1, DM], F32, kind="ExternalInput")
    wmod_d = dt("wmod", [DM, 6 * DM], F32, kind="ExternalInput")
    out_d = dt("out", [TOK, DM], F32, kind="ExternalOutput")

    brep_d = dt("brep", [H * SROW], BF16, kind="Internal")
    w1ts_d = dt("w1ts", [DC * 128 * DFF], BF16, kind="Internal")
    w2ts_d = dt("w2ts", [DC * 128 * DFF], BF16, kind="Internal")
    w3ts_d = dt("w3ts", [(DFF // 128) * 128 * DM], BF16, kind="Internal")
    rowtmp_d = dt("rowtmp", [8 * DM], F32, kind="Internal")
    ctmp_d = dt("ctmp", [DM], F32, kind="Internal")
    dent_d = dt("dent", [2 * WIN], F32, kind="Internal")
    vg_d = dt("vglob", [TT_ALL * 128 * DM], BF16, kind="Internal")

    from contextlib import ExitStack
    ctx = ExitStack()

    def sb(name, shape, dtp):
        return ctx.enter_context(nc.sbuf_tensor(name, shape, dtp))

    def psum(name, shape):
        return ctx.enter_context(nc.psum_tensor(name, shape, F32))

    marks = {}

    def mark(name):
        marks[name] = nc.sbuf_base

    def rewind(name):
        nc.sbuf_base = marks[name]

    GUARD = sb("GUARD", [128, 4096], F32)
    _pv_mark = nc.sbuf_base
    IDEN = sb("IDEN", [128, 128], F32)
    _pv_end = nc.sbuf_base
    nc.sbuf_base = _pv_mark
    PROBEVIEW = sb("PROBEVIEW", [128, 3072], F32)
    nc.sbuf_base = max(_pv_end, nc.sbuf_base)
    nc.sbuf_base = _pv_end
    MR = sb("MR", [1, 256], F32)
    SINKR = sb("SINKR", [1, H], F32)
    CAC = sb("CAC", [128, DC], F32)
    C16 = sb("C16", [16, 128], F32)
    MODC = sb("MODC", [128, 2 * DC], F32)
    DOGS = sb("DOGS", [H, 7], F32)
    DER = sb("DER", [H, 12], F32)

    PGF = sb("PGF", [128, 1], F32)
    PGB = sb("PGB", [128, 1], BF16)
    STAT = sb("STAT", [128, 8], F32)
    ABSW = sb("ABSW", [128, 192], F32)
    SCAL = sb("SCAL", [128, 4], F32)
    QS = sb("QS", [128, 32], F32)
    QRR = sb("QRR", [128, 32], F32)
    QD = sb("QD", [128, 32], F32)
    FQ = sb("FQ", [128, 4], F32)
    FQR = sb("FQR", [128, 4], F32)
    FQD = sb("FQD", [128, 4], F32)
    HQT = sb("HQT", [128, 512], F32)
    SILT = sb("SILT", [128, 256], BF16)
    Y2T = sb("Y2T", [128, 256], BF16)
    HPT = sb("HPT", [128, 256], BF16)
    RND = sb("RND", [128, 256], F32)
    CONSTS = sb("CONSTS", [128, 2], F32)

    mark("A")
    HFM = sb("HFM", [128, HFMW], BF16)
    a_end = nc.sbuf_base
    rewind("A")
    H2Q = sb("H2Q", [128, OFMW], BF16)
    mark("A2")
    HQD = sb("HQD", [128, 8 * TOK], BF16)
    rewind("A2")
    CB1B = sb("CB1B", [128, DM], F32)
    SBB = sb("SBB", [128, DM], F32)
    nc.sbuf_base = max(a_end, nc.sbuf_base)

    mark("B")
    OFM = sb("OFM", [128, OFMW], BF16)
    b_end = nc.sbuf_base
    rewind("B")
    ACC1 = sb("ACC1", [128, 4 * DM], F32)
    b_end = max(b_end, nc.sbuf_base)
    rewind("B")
    WMOD = [sb(f"WMOD{i}", [128, DC * 256], F32) for i in range(2)]
    nc.sbuf_base = max(b_end, nc.sbuf_base)

    mark("C")
    WQKV = sb("WQKV", [128, 2 * DC * 128], BF16)
    WREP0 = sb("WREP0", [128, UW], BF16)
    WREP1 = sb("WREP1", [128, UW], BF16)
    QSB = sb("QSB", [128, TOK], BF16)
    KSB = sb("KSB", [128, TOK + PREV], BF16)
    ESB = [sb(f"ESB{i}", [128, WIN], BF16) for i in range(4)]
    PTSB = [sb(f"PTSB{i}", [128, WIN], BF16) for i in range(4)]
    VSB = sb("VSB", [128, TT_ALL * 130], BF16)
    c_end = nc.sbuf_base
    rewind("C")
    WOS0 = sb("WOS0", [128, DC * 512], BF16)
    c_end = max(c_end, nc.sbuf_base)
    rewind("C")
    ACC2 = sb("ACC2", [128, 4 * DM], F32)
    nc.sbuf_base = max(c_end, nc.sbuf_base)

    mark("D")
    PN = [sb(f"PN{i}", [128, 2048], F32) for i in range(2)]
    d_end = nc.sbuf_base
    rewind("D")
    W1S = [sb(f"W1S{i}", [128, DC * 256], BF16) for i in range(2)]
    W2S = [sb(f"W2S{i}", [128, DC * 256], BF16) for i in range(2)]
    w_end = nc.sbuf_base
    rewind("D")
    OUTS = sb("OUTS", [128, DM], F32)
    nc.sbuf_base = d_end
    mark("D2")
    PSG = [sb(f"PSG{i}", [128, 2048], BF16) for i in range(2)]
    d2_end = nc.sbuf_base
    rewind("D2")
    QT = [sb(f"QT{i}", [128, 1024], F32) for i in range(2)]
    nc.sbuf_base = max(d2_end, w_end, nc.sbuf_base)

    XPST = [sb(f"XPST{i}", [128, DM], F32) for i in range(2)]

    mark("F")
    JUNK = [sb(f"JUNK{i}", [128, DM], F32) for i in range(2)]
    f_end = nc.sbuf_base
    rewind("F")
    W3B = [sb(f"W3B{i}", [128, 8 * 512], BF16) for i in range(2)]
    nc.sbuf_base = max(f_end, nc.sbuf_base)

    mark("H")
    FTW = sb("FTW", [H, 512], F32)
    FTW2 = sb("FTW2", [H, 512], F32)
    DIST = sb("DIST", [H, 512], F32)
    FTAB = sb("FTAB", [H, FW], BF16)
    h_end = nc.sbuf_base
    rewind("H")
    RCB = sb("RCB", [128, WIN], F32)
    DEN = sb("DEN", [1, 2 * WIN], F32)
    h_end = max(h_end, nc.sbuf_base)
    rewind("H")
    GAB = sb("GAB", [128, DM], F32)
    nc.sbuf_base = max(h_end, nc.sbuf_base)

    assert nc.sbuf_base <= nc.sbuf_top, \
        f"SBUF overflow: {nc.sbuf_base} > {nc.sbuf_top}"

    B = [psum(f"bank{i}", [128, 512]) for i in range(8)]
    pb_save = nc.psum_base
    nc.psum_base = pb_save - 2
    BB = {6: ctx.enter_context(nc.psum_tensor("bank6b", [128, 1024], BF16)),
          7: ctx.enter_context(nc.psum_tensor("bank7b", [128, 1024], BF16))}
    nc.psum_base = pb_save

    S = Sch()
    for name in ["dy", "dg", "t", "s", "v", "g", "st", "st2", "dbg",
                 "dwm0", "dwm1", "dm0", "dm", "dp0", "dp1", "dq0", "dq1",
                 "dx0", "dx1", "dk", "dw0", "dw1", "dd", "do0", "do1",
                 "df0", "df1", "d30", "d31", "dv", "dvf", "dvw"]:
        S.handles[name] = ctx.enter_context(nc.semaphore(name))
        S.cnt[name] = 0

    def dma(q, dst, src, inc, waits=()):
        w = list(waits)
        if inc == "dy":
            w.append(("dy", S.cnt["dy"]))
        return S.emit(q, lambda e, d=dst, s_=src: e.dma_start(d, s_), inc=inc,
                      inc_n=16, waits=w)

    dbg_done = []

    def dbg_dump(nm, src_ap, shp, dtp, wts):
        if not dbg:
            return
        if dbg_sel is not None and nm not in dbg_sel:
            return
        od = dt(nm, shp, dtp, kind="ExternalOutput")
        dbg_done.append(dma("y", od.ap(), src_ap, "dbg", waits=wts))

    def em_s(fn, waits):
        return S.emit("s", fn, inc="s", waits=waits)

    def em_v(fn, waits):
        return S.emit("v", fn, inc="v", waits=waits)

    tk = S.emit("g", lambda g: g.iota(IDEN[:], [[1, 128]], base=0,
                                      channel_multiplier=-1,
                                      allow_small_or_imprecise_dtypes=True), inc="g")
    t_iden = em_v(lambda v: v.tensor_single_scalar(IDEN[:], IDEN[:], 0.0,
                                                   ALU.is_equal), [tk])
    t_dist = S.emit("g", lambda g: g.iota(DIST[:], [[1, 512]], base=0,
                                          channel_multiplier=0,
                                          allow_small_or_imprecise_dtypes=True),
                    inc="g")
    t_cm = em_v(lambda v: v.memset(CONSTS[:, 0:1], -MSTAB), [])
    t_ce = em_v(lambda v: v.memset(CONSTS[:, 1:2], LN_EPS), [t_cm])
    t_dog = dma("y", DOGS[:], dog_d.ap(), "dy")
    t_sink = dma("y", SINKR[:], AP(sink_d, 0, [[DM * 0 + H, 1], [1, H]]), "dy")
    t_pgf = dma("y", PGF[:], pg_d.ap(), "dy")
    t_pg = em_v(lambda v: v.tensor_copy(PGB[:], PGF[:]), [t_pgf])
    t_sinke = em_s(lambda s: s.activation(SINKR[:], SINKR[:], AF.Exp,
                                          bias=CONSTS[0:1, 0:1]), [t_sink, t_ce])

    t_scd = None
    for cc in range(DM // 256):
        tld = dma("y", MR[:, 0:256],
                  AP(c_d, cc * 256, [[256, 1], [1, 256]]), "dy",
                  waits=[t_scd] if t_scd else [])
        tsl = em_s(lambda s: s.activation(MR[:, 0:256], MR[:, 0:256], AF.Silu),
                   [tld])
        t_scd = dma("y", AP(ctmp_d, cc * 256, [[256, 1], [1, 256]]),
                    MR[:, 0:256], "dy", waits=[tsl])
    t_c16 = dma("y", C16[:], AP(ctmp_d, 0, [[128, 16], [1, 128]]), "dy",
                waits=[t_scd])
    t_ct = S.emit("t", lambda t: t.transpose(B[5][:, 0:16], C16[:], IDEN[0:16, 0:16]),
                  inc="t", waits=[t_c16, t_iden])
    t_cac = em_v(lambda v: v.tensor_copy(CAC[:], B[5][:, 0:16]), [t_ct])

    d_ = DER

    def softplus(dst, src, tok_in, tmp):
        u1 = em_s(lambda s: s.activation(tmp, src, AF.Abs), [tok_in])
        u2 = em_s(lambda s: s.activation(tmp, tmp, AF.Exp, scale=-1.0), [u1])
        u3 = em_v(lambda v: v.tensor_scalar(tmp, tmp, 1.0, None, ALU.add), [u2])
        u4 = em_s(lambda s: s.activation(tmp, tmp, AF.Ln), [u3])
        u5 = em_v(lambda v: v.tensor_scalar(dst, src, 0.0, None, ALU.max),
                  [tok_in])
        return em_v(lambda v: v.tensor_add(dst, dst, tmp), [u4, u5])

    t1 = softplus(d_[:, 0:1], DOGS[:, 1:2], t_dog, d_[:, 10:11])
    t2 = softplus(d_[:, 1:2], DOGS[:, 4:5], t_dog, d_[:, 11:12])
    t3 = em_s(lambda s: s.activation(d_[:, 2:3], d_[:, 0:1], AF.Square), [t1])
    t4 = em_s(lambda s: s.activation(d_[:, 3:4], d_[:, 1:2], AF.Square), [t2])
    t5 = em_v(lambda v: v.reciprocal(d_[:, 4:5], d_[:, 2:3]), [t3])
    t6 = em_v(lambda v: v.reciprocal(d_[:, 5:6], d_[:, 3:4]), [t4])
    t7 = em_v(lambda v: v.tensor_scalar_mul(d_[:, 6:7], d_[:, 4:5], -0.5), [t5])
    t8 = em_v(lambda v: v.tensor_scalar_mul(d_[:, 7:8], d_[:, 5:6], -0.5), [t6])
    t9 = em_v(lambda v: v.tensor_scalar_mul(d_[:, 8:9], DOGS[:, 2:3], -1.0), [t_dog])
    t10 = em_v(lambda v: v.tensor_scalar_mul(d_[:, 9:10], DOGS[:, 5:6], -1.0),
               [t_dog])
    f1 = em_s(lambda s: s.activation(FTW[:], DIST[:], AF.Square, bias=d_[:, 8:9]),
              [t_dist, t9])
    f2 = em_s(lambda s: s.activation(FTW[:], FTW[:], AF.Exp, scale=d_[:, 6:7]),
              [f1, t7])
    f3 = em_v(lambda v: v.tensor_scalar(FTW[:], FTW[:], DOGS[:, 0:1], None,
                                        ALU.mult), [f2])
    f4 = em_s(lambda s: s.activation(FTW2[:], DIST[:], AF.Square, bias=d_[:, 9:10]),
              [t_dist, t10])
    f5 = em_s(lambda s: s.activation(FTW2[:], FTW2[:], AF.Exp, scale=d_[:, 7:8]),
              [f4, t8])
    f6 = em_v(lambda v: v.tensor_scalar(FTW2[:], FTW2[:], DOGS[:, 3:4], None,
                                        ALU.mult), [f5])
    f7 = em_v(lambda v: v.tensor_sub(FTW[:], FTW[:], FTW2[:]), [f3, f6])
    f8 = em_v(lambda v: v.tensor_scalar(FTW[:], FTW[:], DOGS[:, 6:7], None,
                                        ALU.add), [f7])
    f9 = em_v(lambda v: v.memset(FTAB[:], 0.0), [])
    f10 = em_s(lambda s: s.activation(FTAB[:, 511:1023], FTW[:], AF.Exp), [f8, f9])
    t_brep = dma("y", AP(brep_d, 0, [[SROW, H], [FW, 128], [1, FW]]),
                 AP(FTAB, 0, [[FW, H], [0, 128], [1, FW]]), "dy", waits=[f10])

    NMOD = 48
    mod_ev = []
    wm_load = [None, None]
    wm_mm = [None, None]
    prev_ev = None

    def emit_wmod_load(n):
        src = AP(wmod_d, n * 256, [[6 * DM, 128], [128 * 6 * DM, DC], [1, 256]])
        dstap = AP(WMOD[n % 2], 0, [[DC * 256, 128], [256, DC], [1, 256]])
        w = [wm_mm[n % 2]] if wm_mm[n % 2] else []
        wm_load[n % 2] = dma("y", dstap, src, f"dwm{n % 2}", waits=w)

    for n in range(2):
        emit_wmod_load(n)
    for n in range(NMOD):
        tl = wm_load[n % 2]
        mm = None
        for k in range(DC):
            w = [tl, t_cac] + ([prev_ev] if (k == 0 and prev_ev) else [])
            mm = S.emit("t", lambda t, n=n, k=k: t.matmul(
                B[5][0:1, 0:256], CAC[:, k:k + 1],
                AP(WMOD[n % 2], k * 256, [[DC * 256, 128], [1, 256]]),
                start=(k == 0), stop=(k == DC - 1)),
                inc="t", waits=[x for x in w if x])
        wm_mm[n % 2] = mm
        prev_ev = em_v(lambda v, n=n: v.tensor_copy(MR[:, 0:256], B[5][0:1, 0:256]),
                       [mm] + ([mod_ev[-1]] if mod_ev else []))
        tdr = dma("y", AP(rowtmp_d, n * 256, [[256, 1], [1, 256]]), MR[:, 0:256],
                  "dm0" if n < 16 else "dm", waits=[prev_ev])
        prev_ev = tdr
        mod_ev.append(tdr)
        if n + 2 < NMOD:
            emit_wmod_load(n + 2)
    t_mod_mm_done = wm_mm[1]
    t_mod_done = mod_ev[-1]

    dent_chain = [None]
    bank5_chain = [t_cac]

    def repl(dst_ap, vec, toks, extra=()):
        return dma("y", dst_ap, AP(rowtmp_d, vec * DM, [[0, 128], [1, DM]]), "dy",
                   waits=list(toks) + list(extra))

    def col_major(vec, toks, outcol):
        tr = dma("y", C16[:], AP(rowtmp_d, vec * DM, [[128, 16], [1, 128]]), "dy",
                 waits=list(toks) + [bank5_chain[0]])
        tt = S.emit("t", lambda t: t.transpose(B[5][:, 16:32], C16[:],
                                               IDEN[0:16, 0:16]),
                    inc="t", waits=[tr, t_mod_mm_done, t_iden, bank5_chain[0]])
        ev = em_v(lambda v, outcol=outcol: v.tensor_copy(
            MODC[:, outcol:outcol + DC], B[5][:, 16:32]), [tt])
        bank5_chain[0] = ev
        return ev

    t_caC = col_major(1, [mod_ev[15]], 0)
    t_ca1C = em_v(lambda v: v.tensor_scalar(MODC[:, 0:DC], MODC[:, 0:DC], 1.0,
                                            None, ALU.add), [t_caC])
    t_saC = col_major(0, [mod_ev[15]], DC)

    prep_units = []
    for wi, (wd, wts, abscol, NBI, NBJ, ROWD, TSROW) in enumerate([
            (w1_d, w1ts_d, 0, DFF // 512, DM // 512, DM, DFF),
            (w2_d, w2ts_d, 64, DFF // 512, DM // 512, DM, DFF),
            (w3_d, w3ts_d, 128, DM // 512, DFF // 512, DFF, DM)]):
        for bi in range(NBI):
            for bj in range(NBJ):
                prep_units.append((wi, wd, wts, abscol, bi, bj, ROWD, TSROW, NBJ))

    prep_pn_tok = [[], []]
    prep_out_tok = [None, None]
    s_abs_chain = [None]
    bank_tok = {6: None, 7: None}
    abs_rr = [0]
    prep_cursor = [0]

    def emit_prep_unit():
        ui = prep_cursor[0]
        if ui >= len(prep_units):
            return
        prep_cursor[0] += 1
        wi, wd, wts, abscol, bi, bj, ROWD, TSROW, NBJ = prep_units[ui]
        pn = PN[ui % 2]
        psg = PSG[ui % 2]
        bank = 6 + (ui % 2)
        src = AP(wd, (bi * 512) * ROWD + bj * 512,
                 [[ROWD, 128], [128 * ROWD, 4], [1, 512]])
        dstap = AP(pn, 0, [[2048, 128], [512, 4], [1, 512]])
        tl = dma("s", dstap, src, f"dp{ui % 2}", waits=prep_pn_tok[ui % 2])
        abs_rr[0] += 1
        col = abscol + bi * NBJ + bj
        colap = ABSW[:, col:col + 1]
        if ui < 12 or abs_rr[0] % 2 == 0:
            ta = em_v(lambda v, pn=pn, colap=colap: v.tensor_reduce(
                colap, pn[:], AX.X, ALU.add, apply_absolute_value=True), [tl])
        else:
            ta = em_s(lambda s, pn=pn, colap=colap: s.activation(
                JUNK[0][:], pn[:], AF.Abs, accum_out=colap),
                [tl] + ([s_abs_chain[0]] if s_abs_chain[0] else []))
            s_abs_chain[0] = ta
        last_ev = None
        for dcc in range(4):
            tt = None
            for gg in range(4):
                w = [tl, t_iden]
                if gg == 0 and bank_tok[bank]:
                    w.append(bank_tok[bank])
                tt = S.emit("t", lambda t, pn=pn, gg=gg, dcc=dcc, bank=bank:
                            t.transpose(B[bank][:, gg * 128:(gg + 1) * 128],
                                        pn[:, gg * 512 + dcc * 128:
                                           gg * 512 + (dcc + 1) * 128], IDEN[:]),
                            inc="t", waits=w)
            ev_eng = "s" if (ui + dcc) % 2 == 0 else "v"
            w = [tt] + ([prep_out_tok[ui % 2]] if prep_out_tok[ui % 2] else [])
            if ev_eng == "s":
                last_ev = em_s(lambda s, psg=psg, dcc=dcc, bank=bank:
                               s.activation(psg[:, dcc * 512:(dcc + 1) * 512],
                                            B[bank][:], AF.Sign), w)
            else:
                e1 = em_v(lambda v, psg=psg, dcc=dcc, bank=bank:
                          v.tensor_single_scalar(psg[:, dcc * 512:(dcc + 1) * 512],
                                                 B[bank][:], 0.0, ALU.is_ge), w)
                last_ev = em_v(lambda v, psg=psg, dcc=dcc:
                               v.tensor_scalar(psg[:, dcc * 512:(dcc + 1) * 512],
                                               psg[:, dcc * 512:(dcc + 1) * 512],
                                               2.0, -1.0, ALU.mult, ALU.add), [e1])
            bank_tok[bank] = last_ev
        prep_pn_tok[ui % 2] = [last_ev, ta]
        dst = AP(wts, (bj * 4) * 128 * TSROW + bi * 512,
                 [[TSROW, 128], [128 * TSROW, 4], [1, 512]])
        src2 = AP(psg, 0, [[2048, 128], [512, 4], [1, 512]])
        prep_out_tok[ui % 2] = dma("s", dst, src2, f"dq{ui % 2}",
                               waits=[last_ev])

    def ln_stats(x_ap, sc, tok_in, jb):
        ms = STAT[:, sc:sc + 1]
        sq = STAT[:, sc + 1:sc + 2]
        rs = STAT[:, sc + 2:sc + 3]
        nm = STAT[:, sc + 3:sc + 4]
        w0 = [t for t in tok_in if t]
        a1 = em_v(lambda v: v.tensor_reduce(ms, x_ap, AX.X, ALU.add), w0)
        a2 = em_s(lambda s: s.activation(JUNK[jb][:], x_ap, AF.Square,
                                         accum_out=sq), w0)
        a3 = em_v(lambda v: v.tensor_scalar_mul(ms, ms, 1.0 / DM), [a1])
        a4 = em_v(lambda v: v.tensor_scalar_mul(sq, sq, 1.0 / DM), [a2])
        a5 = em_v(lambda v: v.tensor_mul(nm, ms, ms), [a3])
        a6 = em_v(lambda v: v.tensor_sub(sq, sq, nm), [a4, a5])
        a7 = em_s(lambda s: s.activation(rs, sq, AF.Sqrt,
                                         bias=CONSTS[:, 1:2]), [a6, t_ce])
        a8 = em_v(lambda v: v.reciprocal(rs, rs), [a7])
        a9 = em_v(lambda v: v.tensor_mul(nm, rs, rs), [a8])
        a10 = em_v(lambda v: v.tensor_scalar(sq, sq, LN_EPS, None, ALU.add), [a6])
        a11 = em_v(lambda v: v.tensor_mul(nm, nm, sq), [a9, a10])
        a12 = em_v(lambda v: v.tensor_scalar(nm, nm, -0.5, 1.5, ALU.mult, ALU.add),
                   [a11])
        a13 = em_v(lambda v: v.tensor_mul(rs, rs, nm), [a12])
        a14 = em_v(lambda v: v.tensor_mul(nm, ms, rs), [a13])
        a15 = em_v(lambda v: v.tensor_scalar_mul(nm, nm, -1.0), [a14])
        a16 = em_s(lambda s: s.activation(JUNK[jb][:], x_ap, AF.Identity,
                                          bias=nm, scale=rs), [a15, a2])
        return a16

    def h_tile(x_ap, tile_idx, tok_in, jb):
        a16 = ln_stats(x_ap, 4 * (tile_idx % 2), tok_in, jb)
        last = None
        for grp in range(4):
            bank = 6 + (grp % 2)
            tts = []
            for i in range(4):
                dcch = grp * 4 + i
                w = [a16, t_iden]
                if i == 0 and bank_tok[bank]:
                    w.append(bank_tok[bank])
                tts.append(S.emit(
                    "t", lambda t, dcch=dcch, i=i, bank=bank, jb=jb: t.transpose(
                        B[bank][:, i * 128:(i + 1) * 128],
                        JUNK[jb][:, dcch * 128:(dcch + 1) * 128], IDEN[:]),
                    inc="t", waits=w))
            ev = None
            for i in range(4):
                dcch = grp * 4 + i
                ev = em_s(lambda s, dcch=dcch, i=i, bank=bank, tile_idx=tile_idx:
                          s.activation(
                              AP(HFM, dcch * (TOK + PREV) + tile_idx * 128,
                                 [[HFMW, 128], [1, 128]]),
                              B[bank][:, i * 128:(i + 1) * 128], AF.Identity,
                              scale=MODC[:, dcch:dcch + 1],
                              bias=MODC[:, DC + dcch:DC + dcch + 1]),
                          [tts[-1], t_ca1C, t_saC])
            bank_tok[bank] = ev
            last = ev
        return last

    h_last = None
    x_free = [None, None]
    for it in range(TT_ALL):
        buf = XPST[it % 2]
        if it < 4:
            src = AP(xp_d, it * 128 * DM, [[DM, 128], [1, DM]])
        else:
            src = AP(xo_d, (it - 4) * 128 * DM, [[DM, 128], [1, DM]])
        tl = dma("y", buf[:], src, f"dx{it % 2}",
                 waits=[x_free[it % 2]] if x_free[it % 2] else [])
        hl = h_tile(buf[:], it, [tl], it % 2)
        x_free[it % 2] = hl
        h_last = hl
        emit_prep_unit()
    h_done = [h_last]
    dbg_dump("d_hfm", HFM[:], [128, HFMW], BF16, h_done)
    dbg_dump("d_iden1", IDEN[:], [128, 128], F32, list(h_done))

    qkv_free = [None]
    wrep_free = [None, None]
    vsb_free = [None]
    qk_free = [None]
    esb_free = [None] * 4
    bankf = {i: None for i in range(8)}
    bankf[5] = bank5_chain[0]
    attn_last_v = None
    attn_last_t = None

    vst_tok = [[None] * TT_ALL for _ in range(4)]
    vw_last = None
    vg_stage_free = [None, None]
    for n in range(4):
        srcw = AP(wv_d, n * 512, [[DM, 128], [128 * DM, DC], [1, 512]])
        dstw = AP(WOS0, 0, [[DC * 512, 128], [512, DC], [1, 512]])
        tvw = dma("g", dstw, srcw, "dvw", waits=[vw_last] if vw_last else [])
        for m in range(TT_ALL):
            bk = m % 2
            mm = None
            for k in range(DC):
                w = [tvw] + (h_done if k == 0 else []) + \
                    ([bankf[bk]] if (k == 0 and bankf[bk]) else [])
                mm = S.emit("t", lambda t, k=k, m=m, bk=bk: t.matmul(
                    B[bk][:],
                    AP(HFM, k * (TOK + PREV) + m * 128, [[HFMW, 128], [1, 128]]),
                    WOS0[:, k * 512:(k + 1) * 512],
                    start=(k == 0), stop=(k == DC - 1)),
                    inc="t", waits=[x for x in w if x])
            if m == TT_ALL - 1:
                vw_last = mm
            sc = (m % 2) * 512
            wv_ = [mm, t_pgf] + ([vg_stage_free[m % 2]]
                                 if vg_stage_free[m % 2] else [])
            if m < 4:
                ev = em_v(lambda v, sc=sc, bk=bk: v.tensor_scalar(
                    VSB[:, sc:sc + 512], B[bk][:], PGF[:, 0:1], None,
                    ALU.mult), wv_)
            else:
                ev = em_v(lambda v, sc=sc, bk=bk: v.tensor_copy(
                    VSB[:, sc:sc + 512], B[bk][:]), wv_)
            bankf[bk] = ev
            ts = dma("y", AP(vg_d, (m * 128) * DM + n * 512,
                             [[DM, 128], [1, 512]]),
                     VSB[:, sc:sc + 512], "dv", waits=[ev])
            vst_tok[n][m] = ts
            vg_stage_free[m % 2] = ts
    t_vo = ("dv", S.cnt["dv"])
    for m in range(TT_ALL):
        if m < 4:
            t_vo = em_v(lambda v, m=m: v.tensor_copy(
                AP(VSB, m * 130 + 64, [[TT_ALL * 130, 128], [65, 2], [1, 1]]),
                AP(PGB, 0, [[1, 128], [0, 2], [1, 1]])), [t_vo, t_pg])
        else:
            t_vo = em_v(lambda v, m=m: v.memset(
                AP(VSB, m * 130 + 64, [[TT_ALL * 130, 128], [65, 2], [1, 1]]),
                1.0), [t_vo])
    t_vones = t_vo

    for hp in range(HPAIR):
        tslab = None
        for wi, wd in enumerate((wq_d, wk_d)):
            src = AP(wd, hp * 128, [[DM, 128], [128 * DM, DC], [1, 128]])
            dstap = AP(WQKV, wi * DC * 128, [[2 * DC * 128, 128], [128, DC],
                                             [1, 128]])
            w0 = [qkv_free[0]] if qkv_free[0] else [vw_last]
            tslab = dma("g", dstap, src, "dk", waits=w0)
        if hp == 0:
            dbg_dump("d_idenA", IDEN[:], [128, 128], F32, [tslab])
        wrep_toks = []
        for hh in range(2):
            wr = (WREP0, WREP1)[hh]
            src = AP(brep_d, (2 * hp + hh) * SROW + 127, [[FW - 1, 128], [1, UW]])
            tw = dma("y", wr[:], src, f"dw{hh}",
                     waits=([wrep_free[hh], t_brep] if wrep_free[hh]
                            else [t_brep, vw_last]))
            wrep_toks.append(tw)
        if hp == 0:
            dbg_dump("d_idenB", IDEN[:], [128, 128], F32, wrep_toks)

        q_ev = []
        for nch in range(2):
            mm = None
            for k in range(DC):
                w = [tslab] + (h_done if k == 0 else []) + \
                    ([bankf[nch]] if (k == 0 and bankf[nch]) else [])
                mm = S.emit("t", lambda t, k=k, nch=nch: t.matmul(
                    B[nch][:], WQKV[:, k * 128:(k + 1) * 128],
                    AP(HFM, k * (TOK + PREV) + PREV + nch * 512,
                       [[HFMW, 128], [1, 512]]),
                    start=(k == 0), stop=(k == DC - 1)),
                    inc="t", waits=[x for x in w if x])
            ev = em_v(lambda v, nch=nch: v.tensor_copy(
                QSB[:, nch * 512:(nch + 1) * 512], B[nch][:]),
                [mm] + ([qk_free[0]] if qk_free[0] else []))
            bankf[nch] = ev
            q_ev.append(ev)
        k_ev = []
        for nch in range(3):
            bk = nch % 2
            mm = None
            for k in range(DC):
                w = [tslab] + ([bankf[bk]] if (k == 0 and bankf[bk]) else [])
                mm = S.emit("t", lambda t, k=k, nch=nch, bk=bk: t.matmul(
                    B[bk][:], WQKV[:, DC * 128 + k * 128:DC * 128 + (k + 1) * 128],
                    AP(HFM, k * (TOK + PREV) + nch * 512, [[HFMW, 128], [1, 512]]),
                    start=(k == 0), stop=(k == DC - 1)),
                    inc="t", waits=[x for x in w if x])
            ev = em_v(lambda v, nch=nch, bk=bk: v.tensor_copy(
                KSB[:, nch * 512:(nch + 1) * 512], B[bk][:]),
                [mm] + ([qk_free[0]] if qk_free[0] else []))
            bankf[bk] = ev
            k_ev.append(ev)
        v_evs = []
        for m in range(TT_ALL):
            w = [vst_tok[hp // 4][m], t_vones]
            if vsb_free[0]:
                w.append(vsb_free[0])
            fv = dma("y",
                     AP(VSB, m * 130, [[TT_ALL * 130, 128], [65, 2], [1, 64]]),
                     AP(vg_d, (m * 128) * DM + hp * 128,
                        [[DM, 128], [64, 2], [1, 64]]),
                     "dvf", waits=[x for x in w if x])
            v_evs.append(fv)
        qkv_free[0] = k_ev[-1]
        emit_prep_unit()
        emit_prep_unit()

        last_sT_mm = None
        div_toks = []
        for blk in range(2):
            ctx0 = blk * 512
            pv_last = [None, None]

            def emit_pv(kc, hh, tp):
                nonlocal attn_last_t
                pvb = 4 + hh
                w2 = [tp, v_evs[4 * blk + kc]]
                if kc == 0 and bankf[pvb]:
                    w2.append(bankf[pvb])
                mm2 = S.emit("t", lambda t, hh=hh, kc=kc, blk=blk, pvb=pvb:
                             t.matmul(
                                 B[pvb][0:65, :],
                                 AP(VSB, (4 * blk + kc) * 130 + hh * 65,
                                    [[TT_ALL * 130, 128], [1, 65]]),
                                 PTSB[2 * (kc % 2) + hh][:],
                                 start=(kc == 0), stop=(kc == 7)),
                             inc="t", waits=[x for x in w2 if x])
                pv_last[hh] = mm2
                attn_last_t = mm2

            tp_q = {}
            for kc in range(8):
                for hh in range(2):
                    ei = 2 * (kc % 2) + hh
                    ebt = ESB[ei]
                    pbt = PTSB[ei]
                    bk = 2 * (kc % 2) + hh
                    w = [q_ev[blk], k_ev[blk], k_ev[blk + 1], bankf[bk]]
                    mm = S.emit("t", lambda t, hh=hh, kc=kc, bk=bk, ctx0=ctx0,
                                blk=blk: t.matmul(
                                    B[bk][:],
                                    KSB[hh * 64:(hh + 1) * 64,
                                        ctx0 + kc * 128:ctx0 + (kc + 1) * 128],
                                    QSB[hh * 64:(hh + 1) * 64,
                                        blk * 512:(blk + 1) * 512],
                                    start=True, stop=True,
                                    tile_position=(hh * 64, 0)),
                                inc="t", waits=[x for x in w if x])
                    last_sT_mm = mm
                    te = em_s(lambda s, ebt=ebt, bk=bk: s.activation(
                        ebt[:], B[bk][:], AF.Exp, bias=CONSTS[:, 0:1],
                        scale=0.125),
                        [mm, t_ce] + ([esb_free[ei]] if esb_free[ei] else []))
                    bankf[bk] = te
                    u0 = 896 - 128 * kc
                    tp = em_v(lambda v, ebt=ebt, pbt=pbt, hh=hh, u0=u0:
                              v.tensor_mul(pbt[:], ebt[:],
                                           (WREP0, WREP1)[hh][:, u0:u0 + 512]),
                              [te, wrep_toks[hh]])
                    esb_free[ei] = tp
                    tp_q[(kc, hh)] = tp
                if kc >= 1:
                    for hh in range(2):
                        emit_pv(kc - 1, hh, tp_q.pop((kc - 1, hh)))
            for hh in range(2):
                emit_pv(7, hh, tp_q.pop((7, hh)))
            dts = []
            for hh in range(2):
                td = em_v(lambda v, hh=hh, blk=blk: v.tensor_copy(
                    DEN[0:1, hh * WIN:(hh + 1) * WIN],
                    B[4 + hh][64:65, :]),
                    [pv_last[hh], t_brep] +
                    ([dent_chain[0]] if dent_chain[0] else []))
                dts.append(td)
                td = em_v(lambda v, hh=hh, hp=hp: v.tensor_scalar(
                    DEN[0:1, hh * WIN:(hh + 1) * WIN],
                    DEN[0:1, hh * WIN:(hh + 1) * WIN],
                    SINKR[0:1, 2 * hp + hh:2 * hp + hh + 1], None, ALU.add),
                    [dts[-1], t_sinke])
                dts[-1] = td
            trc = em_v(lambda v: v.reciprocal(DEN[0:1, 0:2 * WIN],
                                              DEN[0:1, 0:2 * WIN]), dts)
            tw1 = dma("y", AP(dent_d, 0, [[WIN, 2], [1, WIN]]),
                      DEN[0:1, 0:2 * WIN], "dd",
                      waits=[trc])
            tw2 = dma("y", RCB[:], AP(dent_d, 0, [[WIN, 2], [0, 64], [1, WIN]]),
                      "dd", waits=[tw1])
            dent_chain[0] = tw2
            for hh in range(2):
                tdv = em_v(lambda v, hh=hh, blk=blk, hp=hp: v.tensor_mul(
                    AP(OFM, hh * 64 * OFMW + hp * TOK + blk * 512,
                       [[OFMW, 64], [1, 512]]),
                    B[4 + hh][0:64, :], RCB[hh * 64:(hh + 1) * 64, :]),
                    [tw2, pv_last[hh], t_mod_done])
                bankf[4 + hh] = tdv
                div_toks.append(tdv)
            dent_chain[0] = div_toks[-1]
            emit_prep_unit()
            emit_prep_unit()
        if hp in (0, 4, 8, 12):
            dbg_dump(f"d_idenC{hp}", IDEN[:], [128, 128], F32, [div_toks[-1]])
        if hp == 0 and dbg and (dbg_sel is None or "d0_vsb" in dbg_sel):
            dbg_dump("d0_vsb", VSB[:], [128, TT_ALL * 130], BF16, [div_toks[-1]])
            dbg_dump("d0_qsb", QSB[:], [128, TOK], BF16, [div_toks[-1]])
            dbg_dump("d0_ksb", KSB[:], [128, TOK + PREV], BF16, [div_toks[-1]])
            dbg_dump("d0_pt0", PTSB[0][:], [128, WIN], BF16, [div_toks[-1]])
            dbg_dump("d0_pt1", PTSB[1][:], [128, WIN], BF16, [div_toks[-1]])
            dbg_dump("d0_esb0", ESB[0][:], [128, WIN], BF16, [div_toks[-1]])
            qk_free[0] = dbg_done[-1] if dbg else last_sT_mm
            if dbg:
                vsb_free[0] = dbg_done[-1]
        qk_free[0] = qk_free[0] if (dbg and hp == 0) else last_sT_mm
        wrep_free[0] = wrep_free[1] = div_toks[-1]
        vsb_free[0] = attn_last_t
        if not (dbg and hp == 0):
            vsb_free[0] = div_toks[-1]
        attn_last_v = div_toks[-1]
        emit_prep_unit()

    while prep_cursor[0] < len(prep_units):
        emit_prep_unit()

    attn_dbg_start = len(dbg_done)
    dbg_dump("d_qsb", QSB[:], [128, TOK], BF16, [attn_last_v])
    dbg_dump("d_ksb", KSB[:], [128, TOK + PREV], BF16, [attn_last_v])
    dbg_dump("d_vsb", VSB[:], [128, TT_ALL * 130], BF16, [attn_last_v])
    dbg_dump("d_ofm", OFM[:], [128, OFMW], BF16, [attn_last_v])
    dbg_dump("d_den", DEN[:], [1, 2 * WIN], F32, [attn_last_v])
    dbg_dump("d_wrep", WREP0[:], [128, UW], BF16, [attn_last_v])

    dbg_dump("d_iden2", IDEN[:], [128, 128], F32, [attn_last_v])

    attn_dbg_done = dbg_done[attn_dbg_start:] if dbg else []

    abs_done = [("v", S.cnt["v"]), ("s", S.cnt["s"]), ("g", S.cnt["g"])]

    def wscale(colbase, out_col, denom):
        r1 = em_v(lambda v: v.tensor_reduce(ABSW[:, colbase:colbase + 1],
                                            ABSW[:, colbase:colbase + 64],
                                            AX.X, ALU.add), abs_done)
        w = [r1, t_iden] + ([bank_tok[6]] if bank_tok[6] else [])
        r2 = S.emit("t", lambda t, colbase=colbase, out_col=out_col: t.transpose(
            B[6][0:1, out_col * 128:(out_col + 1) * 128],
            ABSW[:, colbase:colbase + 1], IDEN[:]),
            inc="t", waits=w)
        r3 = em_v(lambda v, out_col=out_col: v.tensor_reduce(
            MR[:, 8 + out_col:9 + out_col],
            B[6][0:1, out_col * 128:(out_col + 1) * 128], AX.X, ALU.add), [r2])
        r4 = em_v(lambda v, out_col=out_col, denom=denom: v.tensor_scalar_mul(
            MR[:, 8 + out_col:9 + out_col], MR[:, 8 + out_col:9 + out_col],
            1.0 / denom), [r3])
        bank_tok[6] = r4
        return r4

    t_s1 = wscale(0, 0, float(DFF * DM))
    t_s2 = wscale(64, 1, float(DFF * DM))
    t_s3 = wscale(128, 2, float(DM * DFF))
    t_sw = dma("y", AP(rowtmp_d, 6 * DM, [[3, 1], [1, 3]]), MR[0:1, 8:11], "dy",
               waits=[t_s1, t_s2, t_s3])
    t_sr = dma("y", SCAL[:, 0:3], AP(rowtmp_d, 6 * DM, [[0, 128], [1, 3]]), "dy",
               waits=[t_sw])
    t_s1 = t_s2 = t_s3 = t_sr

    def clamp_bcast(tile, tok_in):
        last = tok_in
        for chv in range(4):
            slc = slice(chv * 512, (chv + 1) * 512)
            a = em_s(lambda s, slc=slc: s.activation(HQT[:], tile[:, slc],
                                                     AF.Sign), [last])
            b = em_s(lambda s, slc=slc: s.activation(tile[:, slc], tile[:, slc],
                                                     AF.Abs), [a])
            c = em_v(lambda v, slc=slc: v.tensor_scalar_max(
                tile[:, slc], tile[:, slc], 0.25), [b])
            last = em_v(lambda v, slc=slc: v.tensor_mul(tile[:, slc],
                                                        tile[:, slc], HQT[:]),
                        [c])
        return last

    t_gaB = repl(GAB[:], 2, [mod_ev[47]], extra=[attn_last_v])
    t_gaB = clamp_bcast(GAB, t_gaB)
    t_cb1B = repl(CB1B[:], 4, [mod_ev[47]], extra=[attn_last_v])
    t_cb1B = em_v(lambda v: v.tensor_scalar(CB1B[:], CB1B[:], 1.0, None,
                                            ALU.add), [t_cb1B])
    t_sbB = repl(SBB[:], 3, [mod_ev[47]], extra=[attn_last_v])

    wo_free = [None, None]
    owrite = {}
    xs_free = [None, None]
    stg_free = [None, None]
    h2q_done = []
    qtg_free = [None, None]
    hqx_free = [None]

    for n in range(4):
        wos = WOS0
        wpre = [attn_last_v, attn_last_t, qk_free[0]] + attn_dbg_done
        if n == 0:
            wpre.append(("dk", S.cnt["dk"]))
        if wo_free[0]:
            wpre.append(wo_free[0])
        src = AP(wo_d, n * 512, [[DM, 128], [128 * DM, DC], [1, 512]])
        dstap = AP(wos, 0, [[DC * 512, 128], [512, DC], [1, 512]])
        tl = dma("g", dstap, src, f"do{n % 2}", waits=[x for x in wpre if x])
        for m in range(TT_OWN):
            bk = n % 2
            xsl = dma("y", XPST[m % 2][:, n * 512:(n + 1) * 512],
                      AP(xo_d, m * 128 * DM + n * 512, [[DM, 128], [1, 512]]),
                      f"dx{m % 2}",
                      waits=([xs_free[m % 2]] if xs_free[m % 2]
                             else [x_free[m % 2]]))
            mm = None
            for k in range(DC):
                w = [tl] + ([bankf[bk]] if (k == 0 and bankf[bk]) else [])
                mm = S.emit("t", lambda t, k=k, m=m, n=n, wos=wos, bk=bk: t.matmul(
                    B[bk][:],
                    AP(OFM, k * TOK + m * 128, [[OFMW, 128], [1, 128]]),
                    wos[:, k * 512:(k + 1) * 512],
                    start=(k == 0), stop=(k == DC - 1)),
                    inc="t", waits=[x for x in w if x])
            wq_guard = [mm, t_gaB]
            if stg_free[m % 2]:
                wq_guard.append(stg_free[m % 2])
            elif n == 0:
                wq_guard += [prep_out_tok[0], prep_out_tok[1]]
            te = em_v(lambda v, m=m, n=n, bk=bk: v.tensor_mul(
                QT[m % 2][:, 0:512], B[bk][:],
                GAB[:, n * 512:(n + 1) * 512]), wq_guard)
            te = em_v(lambda v, m=m, n=n: v.tensor_add(
                QT[m % 2][:, 0:512], QT[m % 2][:, 0:512],
                XPST[m % 2][:, n * 512:(n + 1) * 512]), [te, xsl])
            bankf[bk] = te
            xs_free[m % 2] = te
            tw = dma("y", AP(out_d, m * 128 * DM + n * 512, [[DM, 128], [1, 512]]),
                     QT[m % 2][:, 0:512], "st", waits=[te])
            stg_free[m % 2] = tw
            owrite[(m, n)] = tw
            if m == TT_OWN - 1:
                wo_free[0] = mm

    dbg_dump("d_iden3", IDEN[:], [128, 128], F32, [("st", S.cnt["st"])])

    x_res = [None, None]
    x1_write = []
    for m in range(TT_OWN):
        w4 = [owrite[(m, n)] for n in range(4)]
        tl = dma("y", XPST[m % 2][:],
                 AP(out_d, m * 128 * DM, [[DM, 128], [1, DM]]), f"dx{m % 2}",
                 waits=[("st", S.cnt["st"])] +
                 (x_res[m % 2] if x_res[m % 2] else []) +
                 ([qtg_free[m % 2]] if qtg_free[m % 2] else []))
        x1_write.append(owrite[(m, 3)])
        wo_all_st = owrite[(TT_OWN - 1, 3)]
        a16 = ln_stats(XPST[m % 2][:], 4 * (m % 2), [tl], m % 2)
        b1 = em_v(lambda v, m=m: v.tensor_mul(JUNK[m % 2][:], JUNK[m % 2][:],
                                              CB1B[:]), [a16, t_cb1B])
        b2 = em_v(lambda v, m=m: v.tensor_add(JUNK[m % 2][:], JUNK[m % 2][:],
                                              SBB[:]), [b1, t_sbB])
        x_res[m % 2] = [a16]
        q1 = em_v(lambda v, m=m: v.tensor_reduce(
            QS[:], JUNK[m % 2][:].rearrange("p (g k) -> p g k", k=64), AX.X,
            ALU.max, apply_absolute_value=True), [b2])
        q2 = em_v(lambda v: v.tensor_scalar_max(QS[:], QS[:], Q_EPS), [q1])
        q3 = em_v(lambda v: v.reciprocal(QD[:], QS[:]), [q2])
        q4 = em_v(lambda v: v.tensor_scalar_mul(QRR[:], QD[:], QB), [q3])
        q5 = em_v(lambda v: v.tensor_scalar_mul(QD[:], QS[:], 1.0 / QB), [q4])
        tr_toks = []
        for half in range(2):
            c1_ = em_v(lambda v, m=m, half=half: v.tensor_tensor(
                QT[half][:], JUNK[m % 2][:, half * 1024:(half + 1) * 1024],
                AP(QRR, half * 16, [[32, 128], [1, 16], [0, 64]]), ALU.mult),
                [q4, b2] + ([qtg_free[half]] if qtg_free[half]
                            else [wo_all_st]))
            c2_ = S.emit("v", lambda v, half=half: v.tensor_scalar(
                QT[half][:], QT[half][:], MAGIC, MAGIC, ALU.add, ALU.subtract),
                inc="v", waits=[c1_])
            qtg_free[half] = c2_
            tr_toks.append(c2_)
        for chq in range(4):
            half = chq // 2
            d1_ = em_v(lambda v, chq=chq, half=half: v.tensor_tensor(
                HQT[:], QT[half][:, (chq % 2) * 512:((chq % 2) + 1) * 512],
                AP(QD, chq * 8, [[32, 128], [1, 8], [0, 64]]), ALU.mult),
                [tr_toks[half], q5] + ([hqx_free[0]] if hqx_free[0] else []))
            bank = 6 + (chq % 2)
            d1b_ = em_v(lambda v: v.tensor_copy(MR[0:1, 11:12], QS[0:1, 0:1]),
                        [d1_])
            tts = []
            for i in range(4):
                w = [d1b_, t_iden]
                if i == 0 and bank_tok[bank]:
                    w.append(bank_tok[bank])
                tts.append(S.emit("t", lambda t, i=i, bank=bank: t.transpose(
                    B[bank][:, i * 128:(i + 1) * 128],
                    HQT[:, i * 128:(i + 1) * 128], IDEN[:]),
                    inc="t", waits=w))
            ev = None
            for i4 in range(4):
                ev = em_s(lambda ss, chq=chq, bank=bank, m=m, i4=i4: ss.activation(
                    AP(H2Q, (chq * 4 + i4) * TOK + m * 128, [[OFMW, 128], [1, 128]]),
                    B[bank][:, i4 * 128:(i4 + 1) * 128], AF.Identity),
                    [tts[-1]])
            bank_tok[bank] = ev
            hqx_free[0] = ev
            h2q_done.append(ev)
        if dbg and m == 1 and (dbg_sel is None or "de_junk" in dbg_sel):
            dbg_dump("de_junk", JUNK[1][:], [128, DM], F32, [h2q_done[-1], b2])
            dbg_dump("de_qt0", QT[0][:], [128, 1024], F32, [h2q_done[-1]])
            dbg_dump("de_qt1", QT[1][:], [128, 1024], F32, [h2q_done[-1]])
            dbg_dump("de_hqt", HQT[:], [128, 512], F32, [h2q_done[-1]])
            dbg_dump("de_qs", QS[:], [128, 32], F32, [h2q_done[-1]])
            dbg_dump("de_qrr", QRR[:], [128, 32], F32, [h2q_done[-1]])
            dbg_dump("de_qd", QD[:], [128, 32], F32, [h2q_done[-1]])
            if dbg_sel is None or "de_gate" in dbg_sel:
                x_res[1] = x_res[1] + [dbg_done[-1]]
                qtg_free[0] = dbg_done[-1]
                qtg_free[1] = dbg_done[-1]
                hqx_free[0] = dbg_done[-1]

    h2q_all = h2q_done[-4:]
    dbg_dump("d_iden4", IDEN[:], [128, 128], F32, h2q_all)
    last_x1_st = ("st", x1_write[-1][1])

    last_wo_ev = max(bankf[0][1], bankf[1][1])
    t_gb3r = repl(GAB[:], 5, [mod_ev[47]], extra=[("v", last_wo_ev)])
    t_gb3r = clamp_bcast(GAB, t_gb3r)
    t_gb3r = em_v(lambda v: v.tensor_scalar(GAB[:], GAB[:], SCAL[:, 2:3], None,
                                            ALU.mult), [t_gb3r, t_s3])

    acc_t = {}
    for m in range(TT_OWN):
        arena = ACC1 if m < 4 else ACC2
        off = (m % 4) * DM
        acc_t[m] = em_v(lambda v, arena=arena, off=off: v.memset(
            arena[:, off:off + DM], 0.0), h2q_all + [attn_last_v, last_x1_st])

    w12_free = [[None, None], [None, None]]
    w3_free = [None, None]
    hqd_free = [None]
    sil_free = [None]
    y2_free = [None]
    hpt_free = [None]
    rnd_free = [None]
    hq_evs = []
    mm3_done = []
    NSL = DFF // 256

    first_guard = x_res[0] + x_res[1] + [last_x1_st] + h2q_all

    for sl in range(NSL):
        par = sl % 2
        lt = []
        for wi, (wts, bufs) in enumerate(((w1ts_d, W1S), (w2ts_d, W2S))):
            bufw = bufs[par]
            src = AP(wts, sl * 256, [[DFF, 128], [128 * DFF, DC], [1, 256]])
            dstap = AP(bufw, 0, [[DC * 256, 128], [256, DC], [1, 256]])
            wpre = [w12_free[wi][par], prep_out_tok[0], prep_out_tok[1]]
            if sl < 2:
                wpre += first_guard
            lt.append(dma("s", dstap, src, f"df{par}",
                          waits=[x for x in wpre if x]))
        for m in range(TT_OWN):
            mm1 = mm2 = None
            for k in range(DC):
                w = [lt[0], lt[1]] + (h2q_all if k == 0 else []) + \
                    ([bankf[0]] if (k == 0 and bankf[0]) else []) + \
                    ([bankf[1]] if (k == 0 and bankf[1]) else [])
                mm1 = S.emit("t", lambda t, k=k, m=m, par=par: t.matmul(
                    B[0][:, 0:256],
                    AP(H2Q, k * TOK + m * 128, [[OFMW, 128], [1, 128]]),
                    W1S[par][:, k * 256:(k + 1) * 256],
                    start=(k == 0), stop=(k == DC - 1)),
                    inc="t", waits=[x for x in w if x])
                mm2 = S.emit("t", lambda t, k=k, m=m, par=par: t.matmul(
                    B[1][:, 0:256],
                    AP(H2Q, k * TOK + m * 128, [[OFMW, 128], [1, 128]]),
                    W2S[par][:, k * 256:(k + 1) * 256],
                    start=(k == 0), stop=(k == DC - 1)),
                    inc="t", waits=[])
            if m == TT_OWN - 1:
                w12_free[0][par] = mm1
                w12_free[1][par] = mm2
            s1_ = em_s(lambda s: s.activation(SILT[:, 0:256], B[0][:, 0:256],
                                              AF.Silu, scale=SCAL[:, 0:1]),
                       [mm1, t_s1] + ([sil_free[0]] if sil_free[0] else []))
            y2_ = em_s(lambda s: s.activation(Y2T[:, 0:256], B[1][:, 0:256],
                                              AF.Identity),
                       [mm2] + ([y2_free[0]] if y2_free[0] else []))
            bankf[0] = s1_
            bankf[1] = y2_
            hp_ = em_v(lambda v: v.tensor_mul(HPT[:, 0:256], SILT[:, 0:256],
                                              Y2T[:, 0:256]), [s1_, y2_])
            sil_free[0] = hp_
            y2_free[0] = hp_
            g1_ = em_v(lambda v: v.tensor_reduce(
                FQ[:, 0:4], HPT[:, 0:256].rearrange("p (g k) -> p g k", k=64),
                AX.X, ALU.max, apply_absolute_value=True), [hp_])
            g2_ = em_v(lambda v: v.tensor_scalar(FQ[:, 0:4], FQ[:, 0:4],
                                                 SCAL[:, 1:2], Q_EPS, ALU.mult,
                                                 ALU.max), [g1_, t_s2])
            g3_ = em_v(lambda v: v.reciprocal(FQD[:, 0:4], FQ[:, 0:4]), [g2_])
            g4_ = em_v(lambda v: v.tensor_scalar_mul(FQR[:, 0:4], FQD[:, 0:4], QB),
                       [g3_])
            g5_ = em_v(lambda v: v.tensor_scalar_mul(FQD[:, 0:4], FQ[:, 0:4],
                                                     1.0 / QB), [g4_])
            g6_ = em_v(lambda v: v.tensor_scalar(FQR[:, 0:4], FQR[:, 0:4],
                                                 SCAL[:, 1:2], None, ALU.mult),
                       [g4_, t_s2])
            t1_ = em_v(lambda v: v.tensor_tensor(
                RND[:, 0:256], HPT[:, 0:256],
                AP(FQR, 0, [[4, 128], [1, 4], [0, 64]]), ALU.mult),
                [g6_, hp_] + ([rnd_free[0]] if rnd_free[0] else [t_gb3r]))
            t2_ = S.emit("v", lambda v: v.tensor_scalar(
                RND[:, 0:256], RND[:, 0:256], MAGIC, MAGIC, ALU.add, ALU.subtract),
                inc="v", waits=[t1_])
            t3_ = S.emit("v", lambda v: v.tensor_tensor(
                HQT[:, 0:256], RND[:, 0:256],
                AP(FQD, 0, [[4, 128], [1, 4], [0, 64]]), ALU.mult),
                inc="v", waits=[t2_, g5_] + ([hpt_free[0]] if hpt_free[0]
                                             else h2q_all))
            rnd_free[0] = t3_
            bank = 6 + (m % 2)
            tts = []
            for i in range(2):
                w = [t3_, t_iden]
                if i == 0 and bank_tok[bank]:
                    w.append(bank_tok[bank])
                tts.append(S.emit("t", lambda t, i=i, bank=bank: t.transpose(
                    B[bank][:, i * 128:(i + 1) * 128],
                    HQT[:, i * 128:(i + 1) * 128], IDEN[:]),
                    inc="t", waits=w))
            wv2 = [tts[-1]]
            if sl % 4 == 0 and m == 0 and hqd_free[0]:
                wv2.append(hqd_free[0])
            ev = em_v(lambda v, bank=bank, m=m, sl=sl: v.tensor_copy(
                AP(HQD, ((sl % 4) * 2) * TOK + m * 128,
                   [[8 * TOK, 128], [TOK, 2], [1, 128]]),
                B[bank][:, 0:256].rearrange("p (a b) -> p a b", a=2)), wv2)
            bank_tok[bank] = ev
            hpt_free[0] = ev
            hq_evs.append(ev)

        if sl % 4 == 3:
            grp = sl // 4
            hq_slab_evs = hq_evs[-32:]
            mm3_last = None
            for n3 in range(4):
                b3 = W3B[n3 % 2]
                src = AP(w3ts_d, (grp * 8) * 128 * DM + n3 * 512,
                         [[DM, 128], [128 * DM, 8], [1, 512]])
                dstap = AP(b3, 0, [[8 * 512, 128], [512, 8], [1, 512]])
                w = [w3_free[n3 % 2], prep_out_tok[0], prep_out_tok[1]]
                if grp == 0 and n3 < 2:
                    w += x_res[0] + x_res[1] + h2q_all
                tw3 = dma("s", dstap, src, f"d3{n3 % 2}",
                          waits=[x for x in w if x])
                for m in range(TT_OWN):
                    bk = 2 + (m % 2)
                    mm = None
                    for k in range(8):
                        w2 = [tw3] + (hq_slab_evs if k == 0 else [])
                        if k == 0 and bankf[bk]:
                            w2 = w2 + [bankf[bk]]
                        mm = S.emit("t", lambda t, k=k, m=m, b3=b3, bk=bk:
                                    t.matmul(
                                        B[bk][:],
                                        AP(HQD, k * TOK + m * 128,
                                           [[8 * TOK, 128], [1, 128]]),
                                        b3[:, k * 512:(k + 1) * 512],
                                        start=(k == 0), stop=(k == 7)),
                                    inc="t", waits=[x for x in w2 if x])
                    arena = ACC1 if m < 4 else ACC2
                    off = (m % 4) * DM + n3 * 512
                    ta = em_v(lambda v, arena=arena, off=off, bk=bk: v.tensor_add(
                        arena[:, off:off + 512], arena[:, off:off + 512],
                        B[bk][:]), [mm, acc_t[m]])
                    bankf[bk] = ta
                    mm3_last = ta
                w3_free[n3 % 2] = mm3_last
            hqd_free[0] = mm3_last
            mm3_done.append(mm3_last)

    prev_out = None
    xf_free = [None, None]
    for m in range(TT_OWN):
        arena = ACC1 if m < 4 else ACC2
        off = (m % 4) * DM
        wx = [mm3_done[-1]] + ([xf_free[m % 2]] if xf_free[m % 2] else
                               x_res[m % 2])
        tx = dma("y", XPST[m % 2][:],
                 AP(out_d, m * 128 * DM, [[DM, 128], [1, DM]]), f"dx{m % 2}",
                 waits=wx)
        w = [mm3_done[-1], t_gb3r, tx] + ([prev_out] if prev_out else
                                          [w12_free[0][0], w12_free[0][1],
                                           w12_free[1][0], w12_free[1][1]])
        tf = em_v(lambda v, arena=arena, off=off: v.tensor_mul(
            OUTS[:], arena[:, off:off + DM], GAB[:]), w)
        tf = em_v(lambda v, m=m: v.tensor_add(OUTS[:], OUTS[:],
                                              XPST[m % 2][:]), [tf])
        xf_free[m % 2] = tf
        prev_out = dma("y", AP(out_d, m * 128 * DM, [[DM, 128], [1, DM]]),
                       OUTS[:], "st2", waits=[tf])

    dbg_dump("d_scal", SCAL[:, 0:3], [128, 3], F32, [t_s3, t_s1, t_s2])

    dbg_dump("d_absw", ABSW[:, 0:192], [128, 192], F32, [t_s3, t_s1, t_s2])
    dbg_dump("d_h2q", H2Q[:], [128, OFMW], BF16, h2q_all + [mm3_done[-1]])
    dbg_dump("d_h2q_p", H2Q[:], [128, OFMW], BF16, h2q_all)
    dbg_dump("d_modc", MODC[:], [128, 2 * DC], F32, [t_ca1C, t_saC])
    dbg_dump("d_sinkr", SINKR[:], [1, H], F32, [t_sinke])
    dbg_dump("d_hqd", HQD[:], [128, 8 * TOK], BF16, [mm3_done[-1]])
    dbg_dump("d_gab", GAB[:], [128, DM], F32, [t_gb3r, mm3_done[-1]])
    dbg_dump("d_acc", ACC1[:], [128, 4 * DM], F32, [mm3_done[-1]])
    dbg_dump("d_iden", IDEN[:], [128, 128], F32, [("st2", S.cnt["st2"])])
    dbg_dump("d_probe", PROBEVIEW[:], [128, 3072], F32, [("st2", S.cnt["st2"])])
    dbg_dump("d_hqtF", HQT[:], [128, 512], F32, [("st2", S.cnt["st2"])])
    dbg_dump("d_qtF", QT[0][:], [128, 1024], F32, [("st2", S.cnt["st2"])])
    if dbg:
        od = dt("d_rowtmp", [6, DM], F32, kind="ExternalOutput")
        dma("y", od.ap(), AP(rowtmp_d, 0, [[DM, 6], [1, DM]]), "dbg",
            waits=[t_mod_done, mm3_done[-1]])

    S.emit("y", lambda y: y.nop(), waits=[("st", S.cnt["st"]),
                                          ("st2", S.cnt["st2"]),
                                          ("dbg", S.cnt["dbg"])])

    with nc.Block() as block:
        @block.tensor
        def _(e):
            for op in S.ops["t"]:
                op(e)

        @block.scalar
        def _(e):
            for op in S.ops["s"]:
                op(e)

        @block.vector
        def _(e):
            for op in S.ops["v"]:
                op(e)

        @block.gpsimd
        def _(e):
            for op in S.ops["g"]:
                op(e)

        @block.sync
        def _(e):
            for op in S.ops["y"]:
                op(e)

    nc._ctx_keepalive = ctx
    return nc


_NC_CACHE = {}


def kernel(L0_attn_x, L0_attn_Wq, L0_attn_Wk, L0_attn_Wv, L0_attn_Wo,
           L0_attn_dog_params, L0_attn_sinks, L0_ffn_w1, L0_ffn_w2, L0_ffn_w3,
           L0_c, L0_wmod):
    x = np.ascontiguousarray(
        np.asarray(L0_attn_x, dtype=np.float32).reshape(8192, DM))
    if "nc" not in _NC_CACHE:
        _NC_CACHE["nc"] = build()
    nc = _NC_CACHE["nc"]
    com = {
        "wq": np.ascontiguousarray(np.asarray(L0_attn_Wq, np.float32)),
        "wk": np.ascontiguousarray(np.asarray(L0_attn_Wk, np.float32)),
        "wv": np.ascontiguousarray(np.asarray(L0_attn_Wv, np.float32)),
        "wo": np.ascontiguousarray(np.asarray(L0_attn_Wo, np.float32)),
        "dog": np.ascontiguousarray(np.asarray(L0_attn_dog_params, np.float32)),
        "sink": np.ascontiguousarray(
            np.asarray(L0_attn_sinks, np.float32).reshape(H, 1)),
        "w1": np.ascontiguousarray(np.asarray(L0_ffn_w1, np.float32)),
        "w2": np.ascontiguousarray(np.asarray(L0_ffn_w2, np.float32)),
        "w3": np.ascontiguousarray(np.asarray(L0_ffn_w3, np.float32)),
        "cvec": np.ascontiguousarray(np.asarray(L0_c, np.float32).reshape(1, DM)),
        "wmod": np.ascontiguousarray(np.asarray(L0_wmod, np.float32)),
    }
    zeros_prev = np.zeros((PREV, DM), np.float32)
    in_maps = []
    for c in range(NCORE):
        m = dict(com)
        m["xo"] = np.ascontiguousarray(x[c * TOK:(c + 1) * TOK])
        m["xp"] = zeros_prev if c == 0 else np.ascontiguousarray(
            x[c * TOK - PREV:c * TOK])
        m["pg"] = np.full((128, 1), 0.0 if c == 0 else 1.0, np.float32)
        in_maps.append(m)
    res = run_bass_kernel_spmd(nc, in_maps, core_ids=list(range(NCORE)))
    out = np.concatenate([res.results[c]["out"] for c in range(NCORE)], axis=0)
    return out.reshape(1, 8192, DM)


# version 09 (final)
# speedup vs baseline: 1.0524x; 1.0524x over previous
import numpy as np
import ml_dtypes
import concourse.bass as bass
import concourse.mybir as mybir
from concourse.bass_utils import run_bass_kernel_spmd

F32 = mybir.dt.float32
BF16 = mybir.dt.bfloat16
AF = mybir.ActivationFunctionType
ALU = mybir.AluOpType
AX = mybir.AxisListType
AP = bass.AP

NCORE = 8
DM = 2048
TOK = 1024
PREV = 512
TT_OWN = 8
TT_ALL = 12
DC = 16
H = 32
HPAIR = 16
WIN = 512
DFF = 8192
MSTAB = 20.0
MAGIC = 1.5 * 2 ** 23
LN_EPS = 1e-5
Q_EPS = 1e-5
QB = 127.0
FW = 1536
UW = 1408
SROW = 128 * FW
HFMW = DC * (TOK + PREV)
OFMW = DC * TOK


class Sch:
    ENGS = ("t", "s", "v", "g", "y")

    def __init__(self):
        self.ops = {e: [] for e in self.ENGS}
        self.cnt = {}
        self.last_wait = {}
        self.handles = {}

    def emit(self, eng, fn, inc=None, inc_n=1, waits=()):
        wl = []
        for w in waits:
            if w is None:
                continue
            sn, val = w
            assert val <= self.cnt[sn], f"future token {sn}={val} > {self.cnt[sn]}"
            key = (eng, sn)
            if self.last_wait.get(key, -1) < val:
                self.last_wait[key] = val
                wl.append((sn, val))
        tok = None
        if inc is not None:
            self.cnt[inc] += inc_n
            tok = (inc, self.cnt[inc])
        hs = self.handles

        def op(e, wl=wl, fn=fn, inc=inc, inc_n=inc_n):
            for sn, val in wl:
                e.wait_ge(hs[sn], val)
            inst = fn(e)
            if inc is not None:
                inst.then_inc(hs[inc], inc_n)
        self.ops[eng].append(op)
        return tok


def build(dbg=False, dbg_sel=None):
    nc = bass.Bass("TRN2", target_bir_lowering=False,
                   dynamic_dma_scratch_size=8192,
                   detect_race_conditions=False)

    dt = nc.dram_tensor
    xo_d = dt("xo", [TOK, DM], F32, kind="ExternalInput")
    xp_d = dt("xp", [PREV, DM], F32, kind="ExternalInput")
    pg_d = dt("pg", [128, 1], F32, kind="ExternalInput")
    wq_d = dt("wq", [DM, DM], F32, kind="ExternalInput")
    wk_d = dt("wk", [DM, DM], F32, kind="ExternalInput")
    wv_d = dt("wv", [DM, DM], F32, kind="ExternalInput")
    wo_d = dt("wo", [DM, DM], F32, kind="ExternalInput")
    dog_d = dt("dog", [H, 7], F32, kind="ExternalInput")
    sink_d = dt("sink", [H, 1], F32, kind="ExternalInput")
    w1_d = dt("w1", [DFF, DM], F32, kind="ExternalInput")
    w2_d = dt("w2", [DFF, DM], F32, kind="ExternalInput")
    w3_d = dt("w3", [DM, DFF], F32, kind="ExternalInput")
    c_d = dt("cvec", [1, DM], F32, kind="ExternalInput")
    wmod_d = dt("wmod", [DM, 6 * DM], F32, kind="ExternalInput")
    out_d = dt("out", [TOK, DM], F32, kind="ExternalOutput")

    brep_d = dt("brep", [H * SROW], BF16, kind="Internal")
    w1ts_d = dt("w1ts", [DC * 128 * DFF], BF16, kind="Internal")
    w2ts_d = dt("w2ts", [DC * 128 * DFF], BF16, kind="Internal")
    w3ts_d = dt("w3ts", [(DFF // 128) * 128 * DM], BF16, kind="Internal")
    rowtmp_d = dt("rowtmp", [8 * DM], F32, kind="Internal")
    ctmp_d = dt("ctmp", [DM], F32, kind="Internal")
    dent_d = dt("dent", [2 * WIN], F32, kind="Internal")
    vg_d = dt("vglob", [TT_ALL * 128 * DM], BF16, kind="Internal")

    from contextlib import ExitStack
    ctx = ExitStack()

    def sb(name, shape, dtp):
        return ctx.enter_context(nc.sbuf_tensor(name, shape, dtp))

    def psum(name, shape):
        return ctx.enter_context(nc.psum_tensor(name, shape, F32))

    marks = {}

    def mark(name):
        marks[name] = nc.sbuf_base

    def rewind(name):
        nc.sbuf_base = marks[name]

    GUARD = sb("GUARD", [128, 4096], F32)
    _pv_mark = nc.sbuf_base
    IDEN = sb("IDEN", [128, 128], F32)
    _pv_end = nc.sbuf_base
    nc.sbuf_base = _pv_mark
    PROBEVIEW = sb("PROBEVIEW", [128, 3072], F32)
    nc.sbuf_base = max(_pv_end, nc.sbuf_base)
    nc.sbuf_base = _pv_end
    MR = sb("MR", [1, 256], F32)
    SINKR = sb("SINKR", [1, H], F32)
    CAC = sb("CAC", [128, DC], F32)
    C16 = sb("C16", [16, 128], F32)
    MODC = sb("MODC", [128, 2 * DC], F32)
    DOGS = sb("DOGS", [H, 7], F32)
    DER = sb("DER", [H, 12], F32)

    PGF = sb("PGF", [128, 1], F32)
    PGB = sb("PGB", [128, 1], BF16)
    STAT = sb("STAT", [128, 8], F32)
    ABSW = sb("ABSW", [128, 192], F32)
    SCAL = sb("SCAL", [128, 4], F32)
    QS = sb("QS", [128, 32], F32)
    QRR = sb("QRR", [128, 32], F32)
    QD = sb("QD", [128, 32], F32)
    FQ = sb("FQ", [128, 4], F32)
    FQR = sb("FQR", [128, 4], F32)
    FQD = sb("FQD", [128, 4], F32)
    HQT = sb("HQT", [128, 512], F32)
    SILT = sb("SILT", [128, 256], BF16)
    Y2T = sb("Y2T", [128, 256], BF16)
    HPT = sb("HPT", [128, 256], BF16)
    RND = sb("RND", [128, 256], F32)
    CONSTS = sb("CONSTS", [128, 2], F32)

    mark("A")
    HFM = sb("HFM", [128, HFMW], BF16)
    a_end = nc.sbuf_base
    rewind("A")
    H2Q = sb("H2Q", [128, OFMW], BF16)
    mark("A2")
    HQD = sb("HQD", [128, 8 * TOK], BF16)
    rewind("A2")
    CB1B = sb("CB1B", [128, DM], F32)
    SBB = sb("SBB", [128, DM], F32)
    nc.sbuf_base = max(a_end, nc.sbuf_base)

    mark("B")
    OFM = sb("OFM", [128, OFMW], BF16)
    b_end = nc.sbuf_base
    rewind("B")
    ACC1 = sb("ACC1", [128, 4 * DM], F32)
    b_end = max(b_end, nc.sbuf_base)
    rewind("B")
    WMOD = [sb(f"WMOD{i}", [128, DC * 256], F32) for i in range(2)]
    nc.sbuf_base = max(b_end, nc.sbuf_base)

    mark("C")
    WQKV = sb("WQKV", [128, 2 * DC * 128], BF16)
    WREP0 = sb("WREP0", [128, UW], BF16)
    WREP1 = sb("WREP1", [128, UW], BF16)
    QSB = sb("QSB", [128, TOK], BF16)
    KSB = sb("KSB", [128, TOK + PREV], BF16)
    ESB = [sb(f"ESB{i}", [128, WIN], BF16) for i in range(4)]
    PTSB = [sb(f"PTSB{i}", [128, WIN], BF16) for i in range(4)]
    VSB = sb("VSB", [128, TT_ALL * 130], BF16)
    c_end = nc.sbuf_base
    rewind("C")
    WOS0 = sb("WOS0", [128, DC * 512], BF16)
    c_end = max(c_end, nc.sbuf_base)
    rewind("C")
    ACC2 = sb("ACC2", [128, 4 * DM], F32)
    nc.sbuf_base = max(c_end, nc.sbuf_base)

    mark("D")
    PN = [sb(f"PN{i}", [128, 2048], F32) for i in range(2)]
    d_end = nc.sbuf_base
    rewind("D")
    W1S = [sb(f"W1S{i}", [128, DC * 256], BF16) for i in range(2)]
    W2S = [sb(f"W2S{i}", [128, DC * 256], BF16) for i in range(2)]
    w_end = nc.sbuf_base
    rewind("D")
    OUTS = sb("OUTS", [128, DM], F32)
    nc.sbuf_base = d_end
    mark("D2")
    PSG = [sb(f"PSG{i}", [128, 2048], BF16) for i in range(2)]
    d2_end = nc.sbuf_base
    rewind("D2")
    QT = [sb(f"QT{i}", [128, 1024], F32) for i in range(2)]
    nc.sbuf_base = max(d2_end, w_end, nc.sbuf_base)

    XPST = [sb(f"XPST{i}", [128, DM], F32) for i in range(2)]

    mark("F")
    JUNK = [sb(f"JUNK{i}", [128, DM], F32) for i in range(2)]
    f_end = nc.sbuf_base
    rewind("F")
    W3B = [sb(f"W3B{i}", [128, 8 * 512], BF16) for i in range(2)]
    nc.sbuf_base = max(f_end, nc.sbuf_base)

    mark("H")
    FTW = sb("FTW", [H, 512], F32)
    FTW2 = sb("FTW2", [H, 512], F32)
    DIST = sb("DIST", [H, 512], F32)
    FTAB = sb("FTAB", [H, FW], BF16)
    h_end = nc.sbuf_base
    rewind("H")
    RCB = sb("RCB", [128, WIN], F32)
    DEN = sb("DEN", [1, 2 * WIN], F32)
    h_end = max(h_end, nc.sbuf_base)
    rewind("H")
    GAB = sb("GAB", [128, DM], F32)
    nc.sbuf_base = max(h_end, nc.sbuf_base)

    assert nc.sbuf_base <= nc.sbuf_top, \
        f"SBUF overflow: {nc.sbuf_base} > {nc.sbuf_top}"

    B = [psum(f"bank{i}", [128, 512]) for i in range(8)]
    pb_save = nc.psum_base
    nc.psum_base = pb_save - 2
    BB = {6: ctx.enter_context(nc.psum_tensor("bank6b", [128, 1024], BF16)),
          7: ctx.enter_context(nc.psum_tensor("bank7b", [128, 1024], BF16))}
    nc.psum_base = pb_save

    S = Sch()
    for name in ["dy", "dg", "t", "s", "v", "g", "st", "st2", "dbg",
                 "dwm0", "dwm1", "dm0", "dm", "dp0", "dp1", "dq0", "dq1",
                 "dx0", "dx1", "dk", "dw0", "dw1", "dd", "do0", "do1",
                 "df0", "df1", "d30", "d31", "dv", "dvf", "dvw"]:
        S.handles[name] = ctx.enter_context(nc.semaphore(name))
        S.cnt[name] = 0

    def dma(q, dst, src, inc, waits=()):
        w = list(waits)
        if inc == "dy":
            w.append(("dy", S.cnt["dy"]))
        return S.emit(q, lambda e, d=dst, s_=src: e.dma_start(d, s_), inc=inc,
                      inc_n=16, waits=w)

    dbg_done = []

    def dbg_dump(nm, src_ap, shp, dtp, wts):
        if not dbg:
            return
        if dbg_sel is not None and nm not in dbg_sel:
            return
        od = dt(nm, shp, dtp, kind="ExternalOutput")
        dbg_done.append(dma("y", od.ap(), src_ap, "dbg", waits=wts))

    def em_s(fn, waits):
        return S.emit("s", fn, inc="s", waits=waits)

    def em_v(fn, waits):
        return S.emit("v", fn, inc="v", waits=waits)

    tk = S.emit("g", lambda g: g.iota(IDEN[:], [[1, 128]], base=0,
                                      channel_multiplier=-1,
                                      allow_small_or_imprecise_dtypes=True), inc="g")
    t_iden = em_v(lambda v: v.tensor_single_scalar(IDEN[:], IDEN[:], 0.0,
                                                   ALU.is_equal), [tk])
    t_dist = S.emit("g", lambda g: g.iota(DIST[:], [[1, 512]], base=0,
                                          channel_multiplier=0,
                                          allow_small_or_imprecise_dtypes=True),
                    inc="g")
    t_cm = em_v(lambda v: v.memset(CONSTS[:, 0:1], -MSTAB), [])
    t_ce = em_v(lambda v: v.memset(CONSTS[:, 1:2], LN_EPS), [t_cm])
    t_dog = dma("y", DOGS[:], dog_d.ap(), "dy")
    t_sink = dma("y", SINKR[:], AP(sink_d, 0, [[DM * 0 + H, 1], [1, H]]), "dy")
    t_pgf = dma("y", PGF[:], pg_d.ap(), "dy")
    t_pg = em_v(lambda v: v.tensor_copy(PGB[:], PGF[:]), [t_pgf])
    t_sinke = em_s(lambda s: s.activation(SINKR[:], SINKR[:], AF.Exp,
                                          bias=CONSTS[0:1, 0:1]), [t_sink, t_ce])

    t_scd = None
    for cc in range(DM // 256):
        tld = dma("y", MR[:, 0:256],
                  AP(c_d, cc * 256, [[256, 1], [1, 256]]), "dy",
                  waits=[t_scd] if t_scd else [])
        tsl = em_s(lambda s: s.activation(MR[:, 0:256], MR[:, 0:256], AF.Silu),
                   [tld])
        t_scd = dma("y", AP(ctmp_d, cc * 256, [[256, 1], [1, 256]]),
                    MR[:, 0:256], "dy", waits=[tsl])
    t_c16 = dma("y", C16[:], AP(ctmp_d, 0, [[128, 16], [1, 128]]), "dy",
                waits=[t_scd])
    t_ct = S.emit("t", lambda t: t.transpose(B[5][:, 0:16], C16[:], IDEN[0:16, 0:16]),
                  inc="t", waits=[t_c16, t_iden])
    t_cac = em_v(lambda v: v.tensor_copy(CAC[:], B[5][:, 0:16]), [t_ct])

    d_ = DER

    def softplus(dst, src, tok_in, tmp):
        u1 = em_s(lambda s: s.activation(tmp, src, AF.Abs), [tok_in])
        u2 = em_s(lambda s: s.activation(tmp, tmp, AF.Exp, scale=-1.0), [u1])
        u3 = em_v(lambda v: v.tensor_scalar(tmp, tmp, 1.0, None, ALU.add), [u2])
        u4 = em_s(lambda s: s.activation(tmp, tmp, AF.Ln), [u3])
        u5 = em_v(lambda v: v.tensor_scalar(dst, src, 0.0, None, ALU.max),
                  [tok_in])
        return em_v(lambda v: v.tensor_add(dst, dst, tmp), [u4, u5])

    t1 = softplus(d_[:, 0:1], DOGS[:, 1:2], t_dog, d_[:, 10:11])
    t2 = softplus(d_[:, 1:2], DOGS[:, 4:5], t_dog, d_[:, 11:12])
    t3 = em_s(lambda s: s.activation(d_[:, 2:3], d_[:, 0:1], AF.Square), [t1])
    t4 = em_s(lambda s: s.activation(d_[:, 3:4], d_[:, 1:2], AF.Square), [t2])
    t5 = em_v(lambda v: v.reciprocal(d_[:, 4:5], d_[:, 2:3]), [t3])
    t6 = em_v(lambda v: v.reciprocal(d_[:, 5:6], d_[:, 3:4]), [t4])
    t7 = em_v(lambda v: v.tensor_scalar_mul(d_[:, 6:7], d_[:, 4:5], -0.5), [t5])
    t8 = em_v(lambda v: v.tensor_scalar_mul(d_[:, 7:8], d_[:, 5:6], -0.5), [t6])
    t9 = em_v(lambda v: v.tensor_scalar_mul(d_[:, 8:9], DOGS[:, 2:3], -1.0), [t_dog])
    t10 = em_v(lambda v: v.tensor_scalar_mul(d_[:, 9:10], DOGS[:, 5:6], -1.0),
               [t_dog])
    f1 = em_s(lambda s: s.activation(FTW[:], DIST[:], AF.Square, bias=d_[:, 8:9]),
              [t_dist, t9])
    f2 = em_s(lambda s: s.activation(FTW[:], FTW[:], AF.Exp, scale=d_[:, 6:7]),
              [f1, t7])
    f3 = em_v(lambda v: v.tensor_scalar(FTW[:], FTW[:], DOGS[:, 0:1], None,
                                        ALU.mult), [f2])
    f4 = em_s(lambda s: s.activation(FTW2[:], DIST[:], AF.Square, bias=d_[:, 9:10]),
              [t_dist, t10])
    f5 = em_s(lambda s: s.activation(FTW2[:], FTW2[:], AF.Exp, scale=d_[:, 7:8]),
              [f4, t8])
    f6 = em_v(lambda v: v.tensor_scalar(FTW2[:], FTW2[:], DOGS[:, 3:4], None,
                                        ALU.mult), [f5])
    f7 = em_v(lambda v: v.tensor_sub(FTW[:], FTW[:], FTW2[:]), [f3, f6])
    f8 = em_v(lambda v: v.tensor_scalar(FTW[:], FTW[:], DOGS[:, 6:7], None,
                                        ALU.add), [f7])
    f9 = em_v(lambda v: v.memset(FTAB[:], 0.0), [])
    f10 = em_s(lambda s: s.activation(FTAB[:, 511:1023], FTW[:], AF.Exp), [f8, f9])
    t_brep = dma("y", AP(brep_d, 0, [[SROW, H], [FW, 128], [1, FW]]),
                 AP(FTAB, 0, [[FW, H], [0, 128], [1, FW]]), "dy", waits=[f10])

    NMOD = 48
    mod_ev = []
    wm_load = [None, None]
    wm_mm = [None, None]
    prev_ev = None

    def emit_wmod_load(n):
        src = AP(wmod_d, n * 256, [[6 * DM, 128], [128 * 6 * DM, DC], [1, 256]])
        dstap = AP(WMOD[n % 2], 0, [[DC * 256, 128], [256, DC], [1, 256]])
        w = [wm_mm[n % 2]] if wm_mm[n % 2] else []
        wm_load[n % 2] = dma("y", dstap, src, f"dwm{n % 2}", waits=w)

    for n in range(2):
        emit_wmod_load(n)
    for n in range(NMOD):
        tl = wm_load[n % 2]
        mm = None
        for k in range(DC):
            w = [tl, t_cac] + ([prev_ev] if (k == 0 and prev_ev) else [])
            mm = S.emit("t", lambda t, n=n, k=k: t.matmul(
                B[5][0:1, 0:256], CAC[:, k:k + 1],
                AP(WMOD[n % 2], k * 256, [[DC * 256, 128], [1, 256]]),
                start=(k == 0), stop=(k == DC - 1)),
                inc="t", waits=[x for x in w if x])
        wm_mm[n % 2] = mm
        prev_ev = em_v(lambda v, n=n: v.tensor_copy(MR[:, 0:256], B[5][0:1, 0:256]),
                       [mm] + ([mod_ev[-1]] if mod_ev else []))
        tdr = dma("y", AP(rowtmp_d, n * 256, [[256, 1], [1, 256]]), MR[:, 0:256],
                  "dm0" if n < 16 else "dm", waits=[prev_ev])
        prev_ev = tdr
        mod_ev.append(tdr)
        if n + 2 < NMOD:
            emit_wmod_load(n + 2)
    t_mod_mm_done = wm_mm[1]
    t_mod_done = mod_ev[-1]

    dent_chain = [None]
    bank5_chain = [t_cac]

    def repl(dst_ap, vec, toks, extra=()):
        return dma("y", dst_ap, AP(rowtmp_d, vec * DM, [[0, 128], [1, DM]]), "dy",
                   waits=list(toks) + list(extra))

    def col_major(vec, toks, outcol):
        tr = dma("y", C16[:], AP(rowtmp_d, vec * DM, [[128, 16], [1, 128]]), "dy",
                 waits=list(toks) + [bank5_chain[0]])
        tt = S.emit("t", lambda t: t.transpose(B[5][:, 16:32], C16[:],
                                               IDEN[0:16, 0:16]),
                    inc="t", waits=[tr, t_mod_mm_done, t_iden, bank5_chain[0]])
        ev = em_v(lambda v, outcol=outcol: v.tensor_copy(
            MODC[:, outcol:outcol + DC], B[5][:, 16:32]), [tt])
        bank5_chain[0] = ev
        return ev

    t_caC = col_major(1, [mod_ev[15]], 0)
    t_ca1C = em_v(lambda v: v.tensor_scalar(MODC[:, 0:DC], MODC[:, 0:DC], 1.0,
                                            None, ALU.add), [t_caC])
    t_saC = col_major(0, [mod_ev[15]], DC)

    prep_units = []
    for wi, (wd, wts, abscol, NBI, NBJ, ROWD, TSROW) in enumerate([
            (w1_d, w1ts_d, 0, DFF // 512, DM // 512, DM, DFF),
            (w2_d, w2ts_d, 64, DFF // 512, DM // 512, DM, DFF),
            (w3_d, w3ts_d, 128, DM // 512, DFF // 512, DFF, DM)]):
        for bi in range(NBI):
            for bj in range(NBJ):
                prep_units.append((wi, wd, wts, abscol, bi, bj, ROWD, TSROW, NBJ))

    prep_pn_tok = [[], []]
    prep_out_tok = [None, None]
    s_abs_chain = [None]
    bank_tok = {6: None, 7: None}
    abs_rr = [0]
    prep_cursor = [0]

    def emit_prep_unit():
        ui = prep_cursor[0]
        if ui >= len(prep_units):
            return
        prep_cursor[0] += 1
        wi, wd, wts, abscol, bi, bj, ROWD, TSROW, NBJ = prep_units[ui]
        pn = PN[ui % 2]
        psg = PSG[ui % 2]
        bank = 6 + (ui % 2)
        src = AP(wd, (bi * 512) * ROWD + bj * 512,
                 [[ROWD, 128], [128 * ROWD, 4], [1, 512]])
        dstap = AP(pn, 0, [[2048, 128], [512, 4], [1, 512]])
        tl = dma("s", dstap, src, f"dp{ui % 2}", waits=prep_pn_tok[ui % 2])
        abs_rr[0] += 1
        col = abscol + bi * NBJ + bj
        colap = ABSW[:, col:col + 1]
        if ui < 12 or abs_rr[0] % 2 == 0:
            ta = em_v(lambda v, pn=pn, colap=colap: v.tensor_reduce(
                colap, pn[:], AX.X, ALU.add, apply_absolute_value=True), [tl])
        else:
            ta = em_s(lambda s, pn=pn, colap=colap: s.activation(
                JUNK[0][:], pn[:], AF.Abs, accum_out=colap),
                [tl] + ([s_abs_chain[0]] if s_abs_chain[0] else []))
            s_abs_chain[0] = ta
        last_ev = None
        for dcc in range(4):
            tt = None
            for gg in range(4):
                w = [tl, t_iden]
                if gg == 0 and bank_tok[bank]:
                    w.append(bank_tok[bank])
                tt = S.emit("t", lambda t, pn=pn, gg=gg, dcc=dcc, bank=bank:
                            t.transpose(B[bank][:, gg * 128:(gg + 1) * 128],
                                        pn[:, gg * 512 + dcc * 128:
                                           gg * 512 + (dcc + 1) * 128], IDEN[:]),
                            inc="t", waits=w)
            ev_eng = "s" if (ui + dcc) % 2 == 0 else "v"
            w = [tt] + ([prep_out_tok[ui % 2]] if prep_out_tok[ui % 2] else [])
            if ev_eng == "s":
                last_ev = em_s(lambda s, psg=psg, dcc=dcc, bank=bank:
                               s.activation(psg[:, dcc * 512:(dcc + 1) * 512],
                                            B[bank][:], AF.Sign), w)
            else:
                e1 = em_v(lambda v, psg=psg, dcc=dcc, bank=bank:
                          v.tensor_single_scalar(psg[:, dcc * 512:(dcc + 1) * 512],
                                                 B[bank][:], 0.0, ALU.is_ge), w)
                last_ev = em_v(lambda v, psg=psg, dcc=dcc:
                               v.tensor_scalar(psg[:, dcc * 512:(dcc + 1) * 512],
                                               psg[:, dcc * 512:(dcc + 1) * 512],
                                               2.0, -1.0, ALU.mult, ALU.add), [e1])
            bank_tok[bank] = last_ev
        prep_pn_tok[ui % 2] = [last_ev, ta]
        dst = AP(wts, (bj * 4) * 128 * TSROW + bi * 512,
                 [[TSROW, 128], [128 * TSROW, 4], [1, 512]])
        src2 = AP(psg, 0, [[2048, 128], [512, 4], [1, 512]])
        prep_out_tok[ui % 2] = dma("y", dst, src2, f"dq{ui % 2}",
                               waits=[last_ev])

    def ln_stats(x_ap, sc, tok_in, jb):
        ms = STAT[:, sc:sc + 1]
        sq = STAT[:, sc + 1:sc + 2]
        rs = STAT[:, sc + 2:sc + 3]
        nm = STAT[:, sc + 3:sc + 4]
        w0 = [t for t in tok_in if t]
        a1 = em_v(lambda v: v.tensor_reduce(ms, x_ap, AX.X, ALU.add), w0)
        a2 = em_s(lambda s: s.activation(JUNK[jb][:], x_ap, AF.Square,
                                         accum_out=sq), w0)
        a3 = em_v(lambda v: v.tensor_scalar_mul(ms, ms, 1.0 / DM), [a1])
        a4 = em_v(lambda v: v.tensor_scalar_mul(sq, sq, 1.0 / DM), [a2])
        a5 = em_v(lambda v: v.tensor_mul(nm, ms, ms), [a3])
        a6 = em_v(lambda v: v.tensor_sub(sq, sq, nm), [a4, a5])
        a7 = em_s(lambda s: s.activation(rs, sq, AF.Sqrt,
                                         bias=CONSTS[:, 1:2]), [a6, t_ce])
        a8 = em_v(lambda v: v.reciprocal(rs, rs), [a7])
        a9 = em_v(lambda v: v.tensor_mul(nm, rs, rs), [a8])
        a10 = em_v(lambda v: v.tensor_scalar(sq, sq, LN_EPS, None, ALU.add), [a6])
        a11 = em_v(lambda v: v.tensor_mul(nm, nm, sq), [a9, a10])
        a12 = em_v(lambda v: v.tensor_scalar(nm, nm, -0.5, 1.5, ALU.mult, ALU.add),
                   [a11])
        a13 = em_v(lambda v: v.tensor_mul(rs, rs, nm), [a12])
        a14 = em_v(lambda v: v.tensor_mul(nm, ms, rs), [a13])
        a15 = em_v(lambda v: v.tensor_scalar_mul(nm, nm, -1.0), [a14])
        a16 = em_s(lambda s: s.activation(JUNK[jb][:], x_ap, AF.Identity,
                                          bias=nm, scale=rs), [a15, a2])
        return a16

    def h_tile(x_ap, tile_idx, tok_in, jb):
        a16 = ln_stats(x_ap, 4 * (tile_idx % 2), tok_in, jb)
        last = None
        for grp in range(4):
            bank = 6 + (grp % 2)
            tts = []
            for i in range(4):
                dcch = grp * 4 + i
                w = [a16, t_iden]
                if i == 0 and bank_tok[bank]:
                    w.append(bank_tok[bank])
                tts.append(S.emit(
                    "t", lambda t, dcch=dcch, i=i, bank=bank, jb=jb: t.transpose(
                        B[bank][:, i * 128:(i + 1) * 128],
                        JUNK[jb][:, dcch * 128:(dcch + 1) * 128], IDEN[:]),
                    inc="t", waits=w))
            ev = None
            for i in range(4):
                dcch = grp * 4 + i
                ev = em_s(lambda s, dcch=dcch, i=i, bank=bank, tile_idx=tile_idx:
                          s.activation(
                              AP(HFM, dcch * (TOK + PREV) + tile_idx * 128,
                                 [[HFMW, 128], [1, 128]]),
                              B[bank][:, i * 128:(i + 1) * 128], AF.Identity,
                              scale=MODC[:, dcch:dcch + 1],
                              bias=MODC[:, DC + dcch:DC + dcch + 1]),
                          [tts[-1], t_ca1C, t_saC])
            bank_tok[bank] = ev
            last = ev
        return last

    h_last = None
    x_free = [None, None]
    for it in range(TT_ALL):
        buf = XPST[it % 2]
        if it < 4:
            src = AP(xp_d, it * 128 * DM, [[DM, 128], [1, DM]])
        else:
            src = AP(xo_d, (it - 4) * 128 * DM, [[DM, 128], [1, DM]])
        tl = dma("y", buf[:], src, f"dx{it % 2}",
                 waits=[x_free[it % 2]] if x_free[it % 2] else [])
        hl = h_tile(buf[:], it, [tl], it % 2)
        x_free[it % 2] = hl
        h_last = hl
        emit_prep_unit()
    h_done = [h_last]
    dbg_dump("d_hfm", HFM[:], [128, HFMW], BF16, h_done)
    dbg_dump("d_iden1", IDEN[:], [128, 128], F32, list(h_done))

    qkv_free = [None]
    wrep_free = [None, None]
    vsb_free = [None]
    qk_free = [None]
    esb_free = [None] * 4
    bankf = {i: None for i in range(8)}
    bankf[5] = bank5_chain[0]
    attn_last_v = None
    attn_last_t = None

    vst_tok = [[None] * TT_ALL for _ in range(4)]
    vw_last = None
    vg_stage_free = [None, None]
    for n in range(4):
        srcw = AP(wv_d, n * 512, [[DM, 128], [128 * DM, DC], [1, 512]])
        dstw = AP(WOS0, 0, [[DC * 512, 128], [512, DC], [1, 512]])
        tvw = dma("g", dstw, srcw, "dvw", waits=[vw_last] if vw_last else [])
        for m in range(TT_ALL):
            bk = m % 2
            mm = None
            for k in range(DC):
                w = [tvw] + (h_done if k == 0 else []) + \
                    ([bankf[bk]] if (k == 0 and bankf[bk]) else [])
                mm = S.emit("t", lambda t, k=k, m=m, bk=bk: t.matmul(
                    B[bk][:],
                    AP(HFM, k * (TOK + PREV) + m * 128, [[HFMW, 128], [1, 128]]),
                    WOS0[:, k * 512:(k + 1) * 512],
                    start=(k == 0), stop=(k == DC - 1)),
                    inc="t", waits=[x for x in w if x])
            if m == TT_ALL - 1:
                vw_last = mm
            sc = (m % 2) * 512
            wv_ = [mm, t_pgf] + ([vg_stage_free[m % 2]]
                                 if vg_stage_free[m % 2] else [])
            if m < 4:
                ev = em_v(lambda v, sc=sc, bk=bk: v.tensor_scalar(
                    VSB[:, sc:sc + 512], B[bk][:], PGF[:, 0:1], None,
                    ALU.mult), wv_)
            else:
                ev = em_v(lambda v, sc=sc, bk=bk: v.tensor_copy(
                    VSB[:, sc:sc + 512], B[bk][:]), wv_)
            bankf[bk] = ev
            ts = dma("y", AP(vg_d, (m * 128) * DM + n * 512,
                             [[DM, 128], [1, 512]]),
                     VSB[:, sc:sc + 512], "dv", waits=[ev])
            vst_tok[n][m] = ts
            vg_stage_free[m % 2] = ts
    t_vo = ("dv", S.cnt["dv"])
    for m in range(TT_ALL):
        if m < 4:
            t_vo = em_v(lambda v, m=m: v.tensor_copy(
                AP(VSB, m * 130 + 64, [[TT_ALL * 130, 128], [65, 2], [1, 1]]),
                AP(PGB, 0, [[1, 128], [0, 2], [1, 1]])), [t_vo, t_pg])
        else:
            t_vo = em_v(lambda v, m=m: v.memset(
                AP(VSB, m * 130 + 64, [[TT_ALL * 130, 128], [65, 2], [1, 1]]),
                1.0), [t_vo])
    t_vones = t_vo

    for hp in range(HPAIR):
        tslab = None
        for wi, wd in enumerate((wq_d, wk_d)):
            src = AP(wd, hp * 128, [[DM, 128], [128 * DM, DC], [1, 128]])
            dstap = AP(WQKV, wi * DC * 128, [[2 * DC * 128, 128], [128, DC],
                                             [1, 128]])
            w0 = [qkv_free[0]] if qkv_free[0] else [vw_last]
            tslab = dma("g", dstap, src, "dk", waits=w0)
        if hp == 0:
            dbg_dump("d_idenA", IDEN[:], [128, 128], F32, [tslab])
        wrep_toks = []
        for hh in range(2):
            wr = (WREP0, WREP1)[hh]
            src = AP(brep_d, (2 * hp + hh) * SROW + 127, [[FW - 1, 128], [1, UW]])
            tw = dma("y", wr[:], src, f"dw{hh}",
                     waits=([wrep_free[hh], t_brep] if wrep_free[hh]
                            else [t_brep, vw_last]))
            wrep_toks.append(tw)
        if hp == 0:
            dbg_dump("d_idenB", IDEN[:], [128, 128], F32, wrep_toks)

        q_ev = []
        for nch in range(2):
            mm = None
            for k in range(DC):
                w = [tslab] + (h_done if k == 0 else []) + \
                    ([bankf[nch]] if (k == 0 and bankf[nch]) else [])
                mm = S.emit("t", lambda t, k=k, nch=nch: t.matmul(
                    B[nch][:], WQKV[:, k * 128:(k + 1) * 128],
                    AP(HFM, k * (TOK + PREV) + PREV + nch * 512,
                       [[HFMW, 128], [1, 512]]),
                    start=(k == 0), stop=(k == DC - 1)),
                    inc="t", waits=[x for x in w if x])
            ev = em_v(lambda v, nch=nch: v.tensor_copy(
                QSB[:, nch * 512:(nch + 1) * 512], B[nch][:]),
                [mm] + ([qk_free[0]] if qk_free[0] else []))
            bankf[nch] = ev
            q_ev.append(ev)
        k_ev = []
        for nch in range(3):
            bk = nch % 2
            mm = None
            for k in range(DC):
                w = [tslab] + ([bankf[bk]] if (k == 0 and bankf[bk]) else [])
                mm = S.emit("t", lambda t, k=k, nch=nch, bk=bk: t.matmul(
                    B[bk][:], WQKV[:, DC * 128 + k * 128:DC * 128 + (k + 1) * 128],
                    AP(HFM, k * (TOK + PREV) + nch * 512, [[HFMW, 128], [1, 512]]),
                    start=(k == 0), stop=(k == DC - 1)),
                    inc="t", waits=[x for x in w if x])
            ev = em_v(lambda v, nch=nch, bk=bk: v.tensor_copy(
                KSB[:, nch * 512:(nch + 1) * 512], B[bk][:]),
                [mm] + ([qk_free[0]] if qk_free[0] else []))
            bankf[bk] = ev
            k_ev.append(ev)
        v_evs = []
        for m in range(TT_ALL):
            w = [vst_tok[hp // 4][m], t_vones]
            if vsb_free[0]:
                w.append(vsb_free[0])
            fv = dma("y",
                     AP(VSB, m * 130, [[TT_ALL * 130, 128], [65, 2], [1, 64]]),
                     AP(vg_d, (m * 128) * DM + hp * 128,
                        [[DM, 128], [64, 2], [1, 64]]),
                     "dvf", waits=[x for x in w if x])
            v_evs.append(fv)
        qkv_free[0] = k_ev[-1]
        emit_prep_unit()
        emit_prep_unit()

        last_sT_mm = None
        div_toks = []
        for blk in range(2):
            ctx0 = blk * 512
            pv_last = [None, None]

            def emit_pv(kc, hh, tp):
                nonlocal attn_last_t
                pvb = 4 + hh
                w2 = [tp, v_evs[4 * blk + kc]]
                if kc == 0 and bankf[pvb]:
                    w2.append(bankf[pvb])
                mm2 = S.emit("t", lambda t, hh=hh, kc=kc, blk=blk, pvb=pvb:
                             t.matmul(
                                 B[pvb][0:65, :],
                                 AP(VSB, (4 * blk + kc) * 130 + hh * 65,
                                    [[TT_ALL * 130, 128], [1, 65]]),
                                 PTSB[2 * (kc % 2) + hh][:],
                                 start=(kc == 0), stop=(kc == 7)),
                             inc="t", waits=[x for x in w2 if x])
                pv_last[hh] = mm2
                attn_last_t = mm2

            tp_q = {}
            for kc in range(8):
                for hh in range(2):
                    ei = 2 * (kc % 2) + hh
                    ebt = ESB[ei]
                    pbt = PTSB[ei]
                    bk = 2 * (kc % 2) + hh
                    w = [q_ev[blk], k_ev[blk], k_ev[blk + 1], bankf[bk]]
                    mm = S.emit("t", lambda t, hh=hh, kc=kc, bk=bk, ctx0=ctx0,
                                blk=blk: t.matmul(
                                    B[bk][:],
                                    KSB[hh * 64:(hh + 1) * 64,
                                        ctx0 + kc * 128:ctx0 + (kc + 1) * 128],
                                    QSB[hh * 64:(hh + 1) * 64,
                                        blk * 512:(blk + 1) * 512],
                                    start=True, stop=True,
                                    tile_position=(hh * 64, 0)),
                                inc="t", waits=[x for x in w if x])
                    last_sT_mm = mm
                    te = em_s(lambda s, ebt=ebt, bk=bk: s.activation(
                        ebt[:], B[bk][:], AF.Exp, bias=CONSTS[:, 0:1],
                        scale=0.125),
                        [mm, t_ce] + ([esb_free[ei]] if esb_free[ei] else []))
                    bankf[bk] = te
                    u0 = 896 - 128 * kc
                    tp = em_v(lambda v, ebt=ebt, pbt=pbt, hh=hh, u0=u0:
                              v.tensor_mul(pbt[:], ebt[:],
                                           (WREP0, WREP1)[hh][:, u0:u0 + 512]),
                              [te, wrep_toks[hh]])
                    esb_free[ei] = tp
                    tp_q[(kc, hh)] = tp
                if kc >= 1:
                    for hh in range(2):
                        emit_pv(kc - 1, hh, tp_q.pop((kc - 1, hh)))
            for hh in range(2):
                emit_pv(7, hh, tp_q.pop((7, hh)))
            dts = []
            for hh in range(2):
                td = em_v(lambda v, hh=hh, blk=blk: v.tensor_copy(
                    DEN[0:1, hh * WIN:(hh + 1) * WIN],
                    B[4 + hh][64:65, :]),
                    [pv_last[hh], t_brep] +
                    ([dent_chain[0]] if dent_chain[0] else []))
                dts.append(td)
                td = em_v(lambda v, hh=hh, hp=hp: v.tensor_scalar(
                    DEN[0:1, hh * WIN:(hh + 1) * WIN],
                    DEN[0:1, hh * WIN:(hh + 1) * WIN],
                    SINKR[0:1, 2 * hp + hh:2 * hp + hh + 1], None, ALU.add),
                    [dts[-1], t_sinke])
                dts[-1] = td
            trc = em_v(lambda v: v.reciprocal(DEN[0:1, 0:2 * WIN],
                                              DEN[0:1, 0:2 * WIN]), dts)
            tw1 = dma("y", AP(dent_d, 0, [[WIN, 2], [1, WIN]]),
                      DEN[0:1, 0:2 * WIN], "dd",
                      waits=[trc])
            tw2 = dma("y", RCB[:], AP(dent_d, 0, [[WIN, 2], [0, 64], [1, WIN]]),
                      "dd", waits=[tw1])
            dent_chain[0] = tw2
            for hh in range(2):
                tdv = em_v(lambda v, hh=hh, blk=blk, hp=hp: v.tensor_mul(
                    AP(OFM, hh * 64 * OFMW + hp * TOK + blk * 512,
                       [[OFMW, 64], [1, 512]]),
                    B[4 + hh][0:64, :], RCB[hh * 64:(hh + 1) * 64, :]),
                    [tw2, pv_last[hh], t_mod_done])
                bankf[4 + hh] = tdv
                div_toks.append(tdv)
            dent_chain[0] = div_toks[-1]
            emit_prep_unit()
            emit_prep_unit()
        if hp in (0, 4, 8, 12):
            dbg_dump(f"d_idenC{hp}", IDEN[:], [128, 128], F32, [div_toks[-1]])
        if hp == 0 and dbg and (dbg_sel is None or "d0_vsb" in dbg_sel):
            dbg_dump("d0_vsb", VSB[:], [128, TT_ALL * 130], BF16, [div_toks[-1]])
            dbg_dump("d0_qsb", QSB[:], [128, TOK], BF16, [div_toks[-1]])
            dbg_dump("d0_ksb", KSB[:], [128, TOK + PREV], BF16, [div_toks[-1]])
            dbg_dump("d0_pt0", PTSB[0][:], [128, WIN], BF16, [div_toks[-1]])
            dbg_dump("d0_pt1", PTSB[1][:], [128, WIN], BF16, [div_toks[-1]])
            dbg_dump("d0_esb0", ESB[0][:], [128, WIN], BF16, [div_toks[-1]])
            qk_free[0] = dbg_done[-1] if dbg else last_sT_mm
            if dbg:
                vsb_free[0] = dbg_done[-1]
        qk_free[0] = qk_free[0] if (dbg and hp == 0) else last_sT_mm
        wrep_free[0] = wrep_free[1] = div_toks[-1]
        vsb_free[0] = attn_last_t
        if not (dbg and hp == 0):
            vsb_free[0] = div_toks[-1]
        attn_last_v = div_toks[-1]
        emit_prep_unit()

    while prep_cursor[0] < len(prep_units):
        emit_prep_unit()

    attn_dbg_start = len(dbg_done)
    dbg_dump("d_qsb", QSB[:], [128, TOK], BF16, [attn_last_v])
    dbg_dump("d_ksb", KSB[:], [128, TOK + PREV], BF16, [attn_last_v])
    dbg_dump("d_vsb", VSB[:], [128, TT_ALL * 130], BF16, [attn_last_v])
    dbg_dump("d_ofm", OFM[:], [128, OFMW], BF16, [attn_last_v])
    dbg_dump("d_den", DEN[:], [1, 2 * WIN], F32, [attn_last_v])
    dbg_dump("d_wrep", WREP0[:], [128, UW], BF16, [attn_last_v])

    dbg_dump("d_iden2", IDEN[:], [128, 128], F32, [attn_last_v])

    attn_dbg_done = dbg_done[attn_dbg_start:] if dbg else []

    abs_done = [("v", S.cnt["v"]), ("s", S.cnt["s"]), ("g", S.cnt["g"])]

    def wscale(colbase, out_col, denom):
        r1 = em_v(lambda v: v.tensor_reduce(ABSW[:, colbase:colbase + 1],
                                            ABSW[:, colbase:colbase + 64],
                                            AX.X, ALU.add), abs_done)
        w = [r1, t_iden] + ([bank_tok[6]] if bank_tok[6] else [])
        r2 = S.emit("t", lambda t, colbase=colbase, out_col=out_col: t.transpose(
            B[6][0:1, out_col * 128:(out_col + 1) * 128],
            ABSW[:, colbase:colbase + 1], IDEN[:]),
            inc="t", waits=w)
        r3 = em_v(lambda v, out_col=out_col: v.tensor_reduce(
            MR[:, 8 + out_col:9 + out_col],
            B[6][0:1, out_col * 128:(out_col + 1) * 128], AX.X, ALU.add), [r2])
        r4 = em_v(lambda v, out_col=out_col, denom=denom: v.tensor_scalar_mul(
            MR[:, 8 + out_col:9 + out_col], MR[:, 8 + out_col:9 + out_col],
            1.0 / denom), [r3])
        bank_tok[6] = r4
        return r4

    t_s1 = wscale(0, 0, float(DFF * DM))
    t_s2 = wscale(64, 1, float(DFF * DM))
    t_s3 = wscale(128, 2, float(DM * DFF))
    t_sw = dma("y", AP(rowtmp_d, 6 * DM, [[3, 1], [1, 3]]), MR[0:1, 8:11], "dy",
               waits=[t_s1, t_s2, t_s3])
    t_sr = dma("y", SCAL[:, 0:3], AP(rowtmp_d, 6 * DM, [[0, 128], [1, 3]]), "dy",
               waits=[t_sw])
    t_s1 = t_s2 = t_s3 = t_sr

    def clamp_bcast(tile, tok_in):
        last = tok_in
        for chv in range(4):
            slc = slice(chv * 512, (chv + 1) * 512)
            a = em_s(lambda s, slc=slc: s.activation(HQT[:], tile[:, slc],
                                                     AF.Sign), [last])
            b = em_s(lambda s, slc=slc: s.activation(tile[:, slc], tile[:, slc],
                                                     AF.Abs), [a])
            c = em_v(lambda v, slc=slc: v.tensor_scalar_max(
                tile[:, slc], tile[:, slc], 0.25), [b])
            last = em_v(lambda v, slc=slc: v.tensor_mul(tile[:, slc],
                                                        tile[:, slc], HQT[:]),
                        [c])
        return last

    t_gaB = repl(GAB[:], 2, [mod_ev[47]], extra=[attn_last_v])
    t_gaB = clamp_bcast(GAB, t_gaB)
    t_cb1B = repl(CB1B[:], 4, [mod_ev[47]], extra=[attn_last_v])
    t_cb1B = em_v(lambda v: v.tensor_scalar(CB1B[:], CB1B[:], 1.0, None,
                                            ALU.add), [t_cb1B])
    t_sbB = repl(SBB[:], 3, [mod_ev[47]], extra=[attn_last_v])

    wo_free = [None, None]
    owrite = {}
    xs_free = [None, None]
    stg_free = [None, None]
    h2q_done = []
    qtg_free = [None, None]
    hqx_free = [None]

    for n in range(4):
        wos = WOS0
        wpre = [attn_last_v, attn_last_t, qk_free[0]] + attn_dbg_done
        if n == 0:
            wpre.append(("dk", S.cnt["dk"]))
        if wo_free[0]:
            wpre.append(wo_free[0])
        src = AP(wo_d, n * 512, [[DM, 128], [128 * DM, DC], [1, 512]])
        dstap = AP(wos, 0, [[DC * 512, 128], [512, DC], [1, 512]])
        tl = dma("g", dstap, src, f"do{n % 2}", waits=[x for x in wpre if x])
        for m in range(TT_OWN):
            bk = n % 2
            xsl = dma("y", XPST[m % 2][:, n * 512:(n + 1) * 512],
                      AP(xo_d, m * 128 * DM + n * 512, [[DM, 128], [1, 512]]),
                      f"dx{m % 2}",
                      waits=([xs_free[m % 2]] if xs_free[m % 2]
                             else [x_free[m % 2]]))
            mm = None
            for k in range(DC):
                w = [tl] + ([bankf[bk]] if (k == 0 and bankf[bk]) else [])
                mm = S.emit("t", lambda t, k=k, m=m, n=n, wos=wos, bk=bk: t.matmul(
                    B[bk][:],
                    AP(OFM, k * TOK + m * 128, [[OFMW, 128], [1, 128]]),
                    wos[:, k * 512:(k + 1) * 512],
                    start=(k == 0), stop=(k == DC - 1)),
                    inc="t", waits=[x for x in w if x])
            wq_guard = [mm, t_gaB]
            if stg_free[m % 2]:
                wq_guard.append(stg_free[m % 2])
            elif n == 0:
                wq_guard += [prep_out_tok[0], prep_out_tok[1]]
            te = em_v(lambda v, m=m, n=n, bk=bk: v.tensor_mul(
                QT[m % 2][:, 0:512], B[bk][:],
                GAB[:, n * 512:(n + 1) * 512]), wq_guard)
            te = em_v(lambda v, m=m, n=n: v.tensor_add(
                QT[m % 2][:, 0:512], QT[m % 2][:, 0:512],
                XPST[m % 2][:, n * 512:(n + 1) * 512]), [te, xsl])
            bankf[bk] = te
            xs_free[m % 2] = te
            tw = dma("y", AP(out_d, m * 128 * DM + n * 512, [[DM, 128], [1, 512]]),
                     QT[m % 2][:, 0:512], "st", waits=[te])
            stg_free[m % 2] = tw
            owrite[(m, n)] = tw
            if m == TT_OWN - 1:
                wo_free[0] = mm

    dbg_dump("d_iden3", IDEN[:], [128, 128], F32, [("st", S.cnt["st"])])

    x_res = [None, None]
    x1_write = []
    for m in range(TT_OWN):
        w4 = [owrite[(m, n)] for n in range(4)]
        tl = dma("y", XPST[m % 2][:],
                 AP(out_d, m * 128 * DM, [[DM, 128], [1, DM]]), f"dx{m % 2}",
                 waits=[("st", S.cnt["st"])] +
                 (x_res[m % 2] if x_res[m % 2] else []) +
                 ([qtg_free[m % 2]] if qtg_free[m % 2] else []))
        x1_write.append(owrite[(m, 3)])
        wo_all_st = owrite[(TT_OWN - 1, 3)]
        a16 = ln_stats(XPST[m % 2][:], 4 * (m % 2), [tl], m % 2)
        b1 = em_v(lambda v, m=m: v.tensor_mul(JUNK[m % 2][:], JUNK[m % 2][:],
                                              CB1B[:]), [a16, t_cb1B])
        b2 = em_v(lambda v, m=m: v.tensor_add(JUNK[m % 2][:], JUNK[m % 2][:],
                                              SBB[:]), [b1, t_sbB])
        x_res[m % 2] = [a16]
        q1 = em_v(lambda v, m=m: v.tensor_reduce(
            QS[:], JUNK[m % 2][:].rearrange("p (g k) -> p g k", k=64), AX.X,
            ALU.max, apply_absolute_value=True), [b2])
        q2 = em_v(lambda v: v.tensor_scalar_max(QS[:], QS[:], Q_EPS), [q1])
        q3 = em_v(lambda v: v.reciprocal(QD[:], QS[:]), [q2])
        q4 = em_v(lambda v: v.tensor_scalar_mul(QRR[:], QD[:], QB), [q3])
        q5 = em_v(lambda v: v.tensor_scalar_mul(QD[:], QS[:], 1.0 / QB), [q4])
        tr_toks = []
        for half in range(2):
            c1_ = em_v(lambda v, m=m, half=half: v.tensor_tensor(
                QT[half][:], JUNK[m % 2][:, half * 1024:(half + 1) * 1024],
                AP(QRR, half * 16, [[32, 128], [1, 16], [0, 64]]), ALU.mult),
                [q4, b2] + ([qtg_free[half]] if qtg_free[half]
                            else [wo_all_st]))
            c2_ = S.emit("v", lambda v, half=half: v.tensor_scalar(
                QT[half][:], QT[half][:], MAGIC, MAGIC, ALU.add, ALU.subtract),
                inc="v", waits=[c1_])
            qtg_free[half] = c2_
            tr_toks.append(c2_)
        for chq in range(4):
            half = chq // 2
            d1_ = em_v(lambda v, chq=chq, half=half: v.tensor_tensor(
                HQT[:], QT[half][:, (chq % 2) * 512:((chq % 2) + 1) * 512],
                AP(QD, chq * 8, [[32, 128], [1, 8], [0, 64]]), ALU.mult),
                [tr_toks[half], q5] + ([hqx_free[0]] if hqx_free[0] else []))
            bank = 6 + (chq % 2)
            d1b_ = em_v(lambda v: v.tensor_copy(MR[0:1, 11:12], QS[0:1, 0:1]),
                        [d1_])
            tts = []
            for i in range(4):
                w = [d1b_, t_iden]
                if i == 0 and bank_tok[bank]:
                    w.append(bank_tok[bank])
                tts.append(S.emit("t", lambda t, i=i, bank=bank: t.transpose(
                    B[bank][:, i * 128:(i + 1) * 128],
                    HQT[:, i * 128:(i + 1) * 128], IDEN[:]),
                    inc="t", waits=w))
            ev = None
            for i4 in range(4):
                ev = em_s(lambda ss, chq=chq, bank=bank, m=m, i4=i4: ss.activation(
                    AP(H2Q, (chq * 4 + i4) * TOK + m * 128, [[OFMW, 128], [1, 128]]),
                    B[bank][:, i4 * 128:(i4 + 1) * 128], AF.Identity),
                    [tts[-1]])
            bank_tok[bank] = ev
            hqx_free[0] = ev
            h2q_done.append(ev)
        if dbg and m == 1 and (dbg_sel is None or "de_junk" in dbg_sel):
            dbg_dump("de_junk", JUNK[1][:], [128, DM], F32, [h2q_done[-1], b2])
            dbg_dump("de_qt0", QT[0][:], [128, 1024], F32, [h2q_done[-1]])
            dbg_dump("de_qt1", QT[1][:], [128, 1024], F32, [h2q_done[-1]])
            dbg_dump("de_hqt", HQT[:], [128, 512], F32, [h2q_done[-1]])
            dbg_dump("de_qs", QS[:], [128, 32], F32, [h2q_done[-1]])
            dbg_dump("de_qrr", QRR[:], [128, 32], F32, [h2q_done[-1]])
            dbg_dump("de_qd", QD[:], [128, 32], F32, [h2q_done[-1]])
            if dbg_sel is None or "de_gate" in dbg_sel:
                x_res[1] = x_res[1] + [dbg_done[-1]]
                qtg_free[0] = dbg_done[-1]
                qtg_free[1] = dbg_done[-1]
                hqx_free[0] = dbg_done[-1]

    h2q_all = h2q_done[-4:]
    dbg_dump("d_iden4", IDEN[:], [128, 128], F32, h2q_all)
    last_x1_st = ("st", x1_write[-1][1])

    last_wo_ev = max(bankf[0][1], bankf[1][1])
    t_gb3r = repl(GAB[:], 5, [mod_ev[47]], extra=[("v", last_wo_ev)])
    t_gb3r = clamp_bcast(GAB, t_gb3r)
    t_gb3r = em_v(lambda v: v.tensor_scalar(GAB[:], GAB[:], SCAL[:, 2:3], None,
                                            ALU.mult), [t_gb3r, t_s3])

    acc_t = {}
    for m in range(TT_OWN):
        arena = ACC1 if m < 4 else ACC2
        off = (m % 4) * DM
        acc_t[m] = em_v(lambda v, arena=arena, off=off: v.memset(
            arena[:, off:off + DM], 0.0), h2q_all + [attn_last_v, last_x1_st])

    w12_free = [[None, None], [None, None]]
    w3_free = [None, None]
    hqd_free = [None]
    sil_free = [None]
    y2_free = [None]
    hpt_free = [None]
    rnd_free = [None]
    hq_evs = []
    mm3_done = []
    NSL = DFF // 256

    first_guard = x_res[0] + x_res[1] + [last_x1_st] + h2q_all

    for sl in range(NSL):
        par = sl % 2
        lt = []
        for wi, (wts, bufs) in enumerate(((w1ts_d, W1S), (w2ts_d, W2S))):
            bufw = bufs[par]
            src = AP(wts, sl * 256, [[DFF, 128], [128 * DFF, DC], [1, 256]])
            dstap = AP(bufw, 0, [[DC * 256, 128], [256, DC], [1, 256]])
            wpre = [w12_free[wi][par], prep_out_tok[0], prep_out_tok[1]]
            if sl < 2:
                wpre += first_guard
            lt.append(dma("y", dstap, src, f"df{par}",
                          waits=[x for x in wpre if x]))
        for m in range(TT_OWN):
            mm1 = mm2 = None
            for k in range(DC):
                w = [lt[0], lt[1]] + (h2q_all if k == 0 else []) + \
                    ([bankf[0]] if (k == 0 and bankf[0]) else []) + \
                    ([bankf[1]] if (k == 0 and bankf[1]) else [])
                mm1 = S.emit("t", lambda t, k=k, m=m, par=par: t.matmul(
                    B[0][:, 0:256],
                    AP(H2Q, k * TOK + m * 128, [[OFMW, 128], [1, 128]]),
                    W1S[par][:, k * 256:(k + 1) * 256],
                    start=(k == 0), stop=(k == DC - 1)),
                    inc="t", waits=[x for x in w if x])
                mm2 = S.emit("t", lambda t, k=k, m=m, par=par: t.matmul(
                    B[1][:, 0:256],
                    AP(H2Q, k * TOK + m * 128, [[OFMW, 128], [1, 128]]),
                    W2S[par][:, k * 256:(k + 1) * 256],
                    start=(k == 0), stop=(k == DC - 1)),
                    inc="t", waits=[])
            if m == TT_OWN - 1:
                w12_free[0][par] = mm1
                w12_free[1][par] = mm2
            s1_ = em_s(lambda s: s.activation(SILT[:, 0:256], B[0][:, 0:256],
                                              AF.Silu, scale=SCAL[:, 0:1]),
                       [mm1, t_s1] + ([sil_free[0]] if sil_free[0] else []))
            y2_ = em_s(lambda s: s.activation(Y2T[:, 0:256], B[1][:, 0:256],
                                              AF.Identity),
                       [mm2] + ([y2_free[0]] if y2_free[0] else []))
            bankf[0] = s1_
            bankf[1] = y2_
            hp_ = em_v(lambda v: v.tensor_mul(HPT[:, 0:256], SILT[:, 0:256],
                                              Y2T[:, 0:256]), [s1_, y2_])
            sil_free[0] = hp_
            y2_free[0] = hp_
            g1_ = em_v(lambda v: v.tensor_reduce(
                FQ[:, 0:4], HPT[:, 0:256].rearrange("p (g k) -> p g k", k=64),
                AX.X, ALU.max, apply_absolute_value=True), [hp_])
            g2_ = em_v(lambda v: v.tensor_scalar(FQ[:, 0:4], FQ[:, 0:4],
                                                 SCAL[:, 1:2], Q_EPS, ALU.mult,
                                                 ALU.max), [g1_, t_s2])
            g3_ = em_v(lambda v: v.reciprocal(FQD[:, 0:4], FQ[:, 0:4]), [g2_])
            g4_ = em_v(lambda v: v.tensor_scalar_mul(FQR[:, 0:4], FQD[:, 0:4], QB),
                       [g3_])
            g5_ = em_v(lambda v: v.tensor_scalar_mul(FQD[:, 0:4], FQ[:, 0:4],
                                                     1.0 / QB), [g4_])
            g6_ = em_v(lambda v: v.tensor_scalar(FQR[:, 0:4], FQR[:, 0:4],
                                                 SCAL[:, 1:2], None, ALU.mult),
                       [g4_, t_s2])
            t1_ = em_v(lambda v: v.tensor_tensor(
                RND[:, 0:256], HPT[:, 0:256],
                AP(FQR, 0, [[4, 128], [1, 4], [0, 64]]), ALU.mult),
                [g6_, hp_] + ([rnd_free[0]] if rnd_free[0] else [t_gb3r]))
            t2_ = S.emit("v", lambda v: v.tensor_scalar(
                RND[:, 0:256], RND[:, 0:256], MAGIC, MAGIC, ALU.add, ALU.subtract),
                inc="v", waits=[t1_])
            t3_ = S.emit("v", lambda v: v.tensor_tensor(
                HQT[:, 0:256], RND[:, 0:256],
                AP(FQD, 0, [[4, 128], [1, 4], [0, 64]]), ALU.mult),
                inc="v", waits=[t2_, g5_] + ([hpt_free[0]] if hpt_free[0]
                                             else h2q_all))
            rnd_free[0] = t3_
            bank = 6 + (m % 2)
            tts = []
            for i in range(2):
                w = [t3_, t_iden]
                if i == 0 and bank_tok[bank]:
                    w.append(bank_tok[bank])
                tts.append(S.emit("t", lambda t, i=i, bank=bank: t.transpose(
                    B[bank][:, i * 128:(i + 1) * 128],
                    HQT[:, i * 128:(i + 1) * 128], IDEN[:]),
                    inc="t", waits=w))
            wv2 = [tts[-1]]
            if sl % 4 == 0 and m == 0 and hqd_free[0]:
                wv2.append(hqd_free[0])
            ev = em_v(lambda v, bank=bank, m=m, sl=sl: v.tensor_copy(
                AP(HQD, ((sl % 4) * 2) * TOK + m * 128,
                   [[8 * TOK, 128], [TOK, 2], [1, 128]]),
                B[bank][:, 0:256].rearrange("p (a b) -> p a b", a=2)), wv2)
            bank_tok[bank] = ev
            hpt_free[0] = ev
            hq_evs.append(ev)

        if sl % 4 == 3:
            grp = sl // 4
            hq_slab_evs = hq_evs[-32:]
            mm3_last = None
            for n3 in range(4):
                b3 = W3B[n3 % 2]
                src = AP(w3ts_d, (grp * 8) * 128 * DM + n3 * 512,
                         [[DM, 128], [128 * DM, 8], [1, 512]])
                dstap = AP(b3, 0, [[8 * 512, 128], [512, 8], [1, 512]])
                w = [w3_free[n3 % 2], prep_out_tok[0], prep_out_tok[1]]
                if grp == 0 and n3 < 2:
                    w += x_res[0] + x_res[1] + h2q_all
                tw3 = dma("y", dstap, src, f"d3{n3 % 2}",
                          waits=[x for x in w if x])
                for m in range(TT_OWN):
                    bk = 2 + (m % 2)
                    mm = None
                    for k in range(8):
                        w2 = [tw3] + (hq_slab_evs if k == 0 else [])
                        if k == 0 and bankf[bk]:
                            w2 = w2 + [bankf[bk]]
                        mm = S.emit("t", lambda t, k=k, m=m, b3=b3, bk=bk:
                                    t.matmul(
                                        B[bk][:],
                                        AP(HQD, k * TOK + m * 128,
                                           [[8 * TOK, 128], [1, 128]]),
                                        b3[:, k * 512:(k + 1) * 512],
                                        start=(k == 0), stop=(k == 7)),
                                    inc="t", waits=[x for x in w2 if x])
                    arena = ACC1 if m < 4 else ACC2
                    off = (m % 4) * DM + n3 * 512
                    ta = em_v(lambda v, arena=arena, off=off, bk=bk: v.tensor_add(
                        arena[:, off:off + 512], arena[:, off:off + 512],
                        B[bk][:]), [mm, acc_t[m]])
                    bankf[bk] = ta
                    mm3_last = ta
                w3_free[n3 % 2] = mm3_last
            hqd_free[0] = mm3_last
            mm3_done.append(mm3_last)

    prev_out = None
    xf_free = [None, None]
    for m in range(TT_OWN):
        arena = ACC1 if m < 4 else ACC2
        off = (m % 4) * DM
        wx = [mm3_done[-1]] + ([xf_free[m % 2]] if xf_free[m % 2] else
                               x_res[m % 2])
        tx = dma("y", XPST[m % 2][:],
                 AP(out_d, m * 128 * DM, [[DM, 128], [1, DM]]), f"dx{m % 2}",
                 waits=wx)
        w = [mm3_done[-1], t_gb3r, tx] + ([prev_out] if prev_out else
                                          [w12_free[0][0], w12_free[0][1],
                                           w12_free[1][0], w12_free[1][1]])
        tf = em_v(lambda v, arena=arena, off=off: v.tensor_mul(
            OUTS[:], arena[:, off:off + DM], GAB[:]), w)
        tf = em_v(lambda v, m=m: v.tensor_add(OUTS[:], OUTS[:],
                                              XPST[m % 2][:]), [tf])
        xf_free[m % 2] = tf
        prev_out = dma("y", AP(out_d, m * 128 * DM, [[DM, 128], [1, DM]]),
                       OUTS[:], "st2", waits=[tf])

    dbg_dump("d_scal", SCAL[:, 0:3], [128, 3], F32, [t_s3, t_s1, t_s2])

    dbg_dump("d_absw", ABSW[:, 0:192], [128, 192], F32, [t_s3, t_s1, t_s2])
    dbg_dump("d_h2q", H2Q[:], [128, OFMW], BF16, h2q_all + [mm3_done[-1]])
    dbg_dump("d_h2q_p", H2Q[:], [128, OFMW], BF16, h2q_all)
    dbg_dump("d_modc", MODC[:], [128, 2 * DC], F32, [t_ca1C, t_saC])
    dbg_dump("d_sinkr", SINKR[:], [1, H], F32, [t_sinke])
    dbg_dump("d_hqd", HQD[:], [128, 8 * TOK], BF16, [mm3_done[-1]])
    dbg_dump("d_gab", GAB[:], [128, DM], F32, [t_gb3r, mm3_done[-1]])
    dbg_dump("d_acc", ACC1[:], [128, 4 * DM], F32, [mm3_done[-1]])
    dbg_dump("d_iden", IDEN[:], [128, 128], F32, [("st2", S.cnt["st2"])])
    dbg_dump("d_probe", PROBEVIEW[:], [128, 3072], F32, [("st2", S.cnt["st2"])])
    dbg_dump("d_hqtF", HQT[:], [128, 512], F32, [("st2", S.cnt["st2"])])
    dbg_dump("d_qtF", QT[0][:], [128, 1024], F32, [("st2", S.cnt["st2"])])
    if dbg:
        od = dt("d_rowtmp", [6, DM], F32, kind="ExternalOutput")
        dma("y", od.ap(), AP(rowtmp_d, 0, [[DM, 6], [1, DM]]), "dbg",
            waits=[t_mod_done, mm3_done[-1]])

    S.emit("y", lambda y: y.nop(), waits=[("st", S.cnt["st"]),
                                          ("st2", S.cnt["st2"]),
                                          ("dbg", S.cnt["dbg"])])

    with nc.Block() as block:
        @block.tensor
        def _(e):
            for op in S.ops["t"]:
                op(e)

        @block.scalar
        def _(e):
            for op in S.ops["s"]:
                op(e)

        @block.vector
        def _(e):
            for op in S.ops["v"]:
                op(e)

        @block.gpsimd
        def _(e):
            for op in S.ops["g"]:
                op(e)

        @block.sync
        def _(e):
            for op in S.ops["y"]:
                op(e)

    nc._ctx_keepalive = ctx
    return nc


_NC_CACHE = {}


def kernel(L0_attn_x, L0_attn_Wq, L0_attn_Wk, L0_attn_Wv, L0_attn_Wo,
           L0_attn_dog_params, L0_attn_sinks, L0_ffn_w1, L0_ffn_w2, L0_ffn_w3,
           L0_c, L0_wmod):
    x = np.ascontiguousarray(
        np.asarray(L0_attn_x, dtype=np.float32).reshape(8192, DM))
    if "nc" not in _NC_CACHE:
        _NC_CACHE["nc"] = build()
    nc = _NC_CACHE["nc"]
    com = {
        "wq": np.ascontiguousarray(np.asarray(L0_attn_Wq, np.float32)),
        "wk": np.ascontiguousarray(np.asarray(L0_attn_Wk, np.float32)),
        "wv": np.ascontiguousarray(np.asarray(L0_attn_Wv, np.float32)),
        "wo": np.ascontiguousarray(np.asarray(L0_attn_Wo, np.float32)),
        "dog": np.ascontiguousarray(np.asarray(L0_attn_dog_params, np.float32)),
        "sink": np.ascontiguousarray(
            np.asarray(L0_attn_sinks, np.float32).reshape(H, 1)),
        "w1": np.ascontiguousarray(np.asarray(L0_ffn_w1, np.float32)),
        "w2": np.ascontiguousarray(np.asarray(L0_ffn_w2, np.float32)),
        "w3": np.ascontiguousarray(np.asarray(L0_ffn_w3, np.float32)),
        "cvec": np.ascontiguousarray(np.asarray(L0_c, np.float32).reshape(1, DM)),
        "wmod": np.ascontiguousarray(np.asarray(L0_wmod, np.float32)),
    }
    zeros_prev = np.zeros((PREV, DM), np.float32)
    in_maps = []
    for c in range(NCORE):
        m = dict(com)
        m["xo"] = np.ascontiguousarray(x[c * TOK:(c + 1) * TOK])
        m["xp"] = zeros_prev if c == 0 else np.ascontiguousarray(
            x[c * TOK - PREV:c * TOK])
        m["pg"] = np.full((128, 1), 0.0 if c == 0 else 1.0, np.float32)
        in_maps.append(m)
    res = run_bass_kernel_spmd(nc, in_maps, core_ids=list(range(NCORE)))
    out = np.concatenate([res.results[c]["out"] for c in range(NCORE)], axis=0)
    return out.reshape(1, 8192, DM)
